# version 02
# baseline (speedup 1.0000x reference)
import numpy as np
import ml_dtypes

import concourse.bass as bass
import concourse.mybir as mybir
from concourse.bass_utils import run_bass_kernel_spmd

F32 = mybir.dt.float32
BF16 = mybir.dt.bfloat16
AF = mybir.ActivationFunctionType
ALU = mybir.AluOpType
AX = mybir.AxisListType

D = 2048
T = 4096
B = 2
H = 16
HKV = 4
NPOOL = 48
NEG = -1.0e9
FFH = 1408
NE = 16
NSLOT = 8
VIS = [[0]] * 6 + [[0, 1, 2, 3, 4], list(range(9))]
SQS = float(1.0 / np.sqrt(32.0))


class _Dummy:
    def then_inc(self, *a, **k):
        return self


_DUM = _Dummy()


class Proxy:
    def __init__(self, eng):
        self.e = eng


class Res:
    __slots__ = ("name", "w", "readers")

    def __init__(self, name):
        self.name = name
        self.w = None
        self.readers = []


class IO:
    def __init__(self, sems, live_name, eng):
        self.cnt = {k: 0 for k in sems}
        self.sems = sems
        self.waited = {k: 0 for k in sems}
        for nm in ["t", "v", "s", "g", "y"]:
            setattr(self, nm, Proxy(eng if nm == live_name else None))
        self.pmap = {}
        self.pmap[id(self.t)] = "pe"
        self.pmap[id(self.v)] = "dve"
        self.pmap[id(self.s)] = "act"
        self.pmap[id(self.g)] = "pool"
        self.pmap[id(self.y)] = "spq"

    def n(self, cname):
        return self.cnt[cname]

    def wait(self, proxy, cname, val):
        if val <= 0:
            return
        if proxy.e is not None:
            if self.waited[cname] >= val:
                return
            proxy.e.wait_ge(self.sems[cname], val)
            self.waited[cname] = val

    def emit(self, proxy, fns, reads=(), writes=(), inc=1, cname=None):
        if cname is None:
            cname = self.pmap[id(proxy)]
        for r in reads:
            if r.w:
                self.wait(proxy, *r.w)
        for r in writes:
            if r.w:
                self.wait(proxy, *r.w)
            for rd in r.readers:
                self.wait(proxy, *rd)
        dma = cname in ("dx", "dxo", "dwa", "dwf", "dmk", "dtr", "dcs", "dou")
        if dma:
            assert inc == 16 * len(fns), (cname, inc, len(fns))
        last = _DUM
        if proxy.e is not None:
            for f in fns:
                last = f()
                if dma and last is not _DUM:
                    last.then_inc(self.sems[cname], 16)
        if (last is not _DUM) and not dma:
            last.then_inc(self.sems[cname], inc)
        self.cnt[cname] += inc
        mark = (cname, self.cnt[cname])
        for r in reads:
            r.readers.append(mark)
        for r in writes:
            r.w = mark
            r.readers = []
        return mark


def build_program(dbg=(), p3=True, p4=True, p3_slots=NSLOT):
    nc = bass.Bass("TRN2", target_bir_lowering=False, debug=False)

    def din(name, shape, dtype=F32):
        return nc.dram_tensor(name, list(shape), dtype, kind="ExternalInput").ap()

    I = {}
    I["xb"] = din("xb", [T, D])
    I["x_own"] = din("x_own", [NSLOT * 128, D])
    I["wqs"] = din("wqs", [D, 512])
    I["wks"] = din("wks", [D, 128])
    I["wqg"] = din("wqg", [D, 512])
    I["wkg"] = din("wkg", [D, 128])
    I["wv"] = din("wv", [D, 256])
    I["wo"] = din("wo", [1024, D])
    I["gw"] = din("gw", [NE, D])
    I["w1"] = din("w1", [NE, D, FFH])
    I["w2"] = din("w2", [NE, FFH, D])
    I["sw1"] = din("sw1", [1, D, FFH])
    I["sw2"] = din("sw2", [1, FFH, D])
    I["lsx"] = din("lsx", [128, 4])
    I["csc"] = din("csc", [128, T], BF16)
    I["css"] = din("css", [128, T], BF16)
    I["cso_c"] = din("cso_c", [128, NSLOT * 128], BF16)
    I["cso_s"] = din("cso_s", [128, NSLOT * 128], BF16)
    I["negid"] = din("negid", [128, 128], BF16)
    I["pswap"] = din("pswap", [128, 128], BF16)
    I["mstd"] = din("mstd", [NSLOT, 9, 128, 128], BF16)
    I["mtt"] = din("mtt", [NSLOT, 9, 128, 128], BF16)
    I["degA"] = din("degA", [128, 128], BF16)
    I["degB"] = din("degB", [128, 128], BF16)
    y_out = nc.dram_tensor("y", [NSLOT * 128, D], F32, kind="ExternalOutput").ap()

    from contextlib import ExitStack
    ctx = ExitStack()
    sball = lambda name, shape, dt: ctx.enter_context(
        nc.sbuf_tensor(name, list(shape), dt))

    y_acc = sball("y_acc", [128, NSLOT * D], F32)
    xn_own = sball("xn_own", [128, 16 * 1024], BF16)
    comb = sball("comb", [128, NSLOT * 16], F32)
    scol = sball("scol", [128, 4], F32)
    ones_col = sball("ones_col", [128, 1], BF16)
    zero_col = sball("zero_col", [128, 1], BF16)
    negid_sb = sball("negid_sb", [128, 128], BF16)
    pswap_sb = sball("pswap_sb", [128, 128], BF16)
    lsx_sb = sball("lsx_sb", [128, 4], F32)
    scr8 = sball("scr8", [128, 8], F32)
    lnscr2 = [sball(f"lnscr{i}", [128, 32], F32) for i in range(2)]
    bn6 = [t[:, 0:24] for t in lnscr2]
    bn2 = [t[:, 24:26] for t in lnscr2]
    bn2_mean = [t[:, 24:25] for t in lnscr2]
    bn2_var = [t[:, 25:26] for t in lnscr2]
    lnv = [t[:, 26:27] for t in lnscr2]
    rstd = [t[:, 27:28] for t in lnscr2]
    epsc = lnscr2[0][:, 29:30]

    ARENA = 112640
    arena = sball("arena", [128, ARENA], mybir.dt.uint8)
    arena_base = nc.lookup_mloc(arena).addr

    def at(name, shape, dtype, off):
        nbytes = int(np.prod(shape[1:])) * mybir.dt.size(dtype)
        assert off + nbytes <= ARENA, f"{name}: {off}+{nbytes} > {ARENA}"
        return nc.alloc_sbuf_tensor_at(name, list(shape), dtype,
                                       offset=arena_base + off)

    wks_sb = at("wks_sb", [128, 16 * 128], BF16, 0)
    wkg_sb = at("wkg_sb", [128, 16 * 128], BF16, 4096)
    wv_sb = at("wv_sb", [128, 16 * 256], BF16, 8192)
    xn_fm2 = [at(f"xn_fm{i}", [128, 16 * 512], BF16, 16384 + 16384 * i)
              for i in range(2)]
    cs_c = at("cs_c", [128, 512], BF16, 49152)
    cs_s = at("cs_s", [128, 512], BF16, 50176)
    kgraw = at("kgraw", [128, 512], BF16, 51200)
    kgsw = at("kgsw", [128, 512], BF16, 52224)
    kgro = at("kgro", [128, 512], BF16, 53248)
    pxn = at("pxn", [128, 16 * NPOOL], BF16, 54272)
    kgp = at("kgp", [128, NPOOL], F32, 55808)
    xstage = [at(f"xst{i}", [128, D], BF16, 56064 + i * 4096) for i in range(2)]
    xnstage = [at(f"xnst{i}", [128, D], BF16, 64256 + i * 4096) for i in range(2)]
    kaug = [at(f"kaug{g}", [65, 9 * 128], BF16, 72448 + g * 2304) for g in range(HKV)]
    v_sb = [at(f"v{c}", [128, 256], BF16, 81664 + c * 512) for c in range(9)]
    gw_sb = at("gw_sb", [16, D], BF16, 0)
    gwt = at("gwt", [128, 256], BF16, 4096)
    sgate = at("sgate", [128, 16], F32, 49152)
    gm1 = at("gm1", [128, 4], F32, 49216)
    grm = at("grm", [128, 4], F32, 49248)
    gs4 = at("gs4", [128, 4], F32, 49280)
    geq = at("geq", [128, 16], F32, 49312)
    gtmp = at("gtmp", [128, 16], F32, 49376)
    gmask = at("gmask", [128, 4], F32, 49440)
    emask = at("emask", [128, 16], F32, 49472)
    gmsk16 = at("gmsk16", [128, 16], F32, 49536)
    t1c = at("t1c", [128, 1], F32, 49600)
    rtc = at("rtc", [128, 1], F32, 49632)
    dinv = at("dinv", [128, 1], F32, 49664)
    wo_sb = at("wo_sb", [128, 8 * D], BF16, 0)
    wqs_sb = at("wqs_sb", [128, 16 * 512], BF16, 32768)
    wqg_sb = at("wqg_sb", [128, 16 * 512], BF16, 49152)
    cso_c_sb = at("cso_c_sb", [128, NSLOT * 128], BF16, 65536)
    cso_s_sb = at("cso_s_sb", [128, NSLOT * 128], BF16, 67584)
    mcst = at("mcst", [128, H * 9], F32, 69632)
    negm = at("negm", [128, H * 9], F32, 70208)
    dcst = at("dcst", [128, H * 9], F32, 70784)
    edst = at("edst", [128, H * 9], F32, 71360)
    mall = at("mall", [128, H], F32, 71936)
    dall = at("dall", [128, H], F32, 72000)
    mprime = at("mprime", [128, H], F32, 72064)
    mprbf = at("mprbf", [128, H], BF16, 72128)
    expT = [at(f"expT{i}", [128, H * 128], BF16, 86272 + i * 4096) for i in range(2)]
    expsc = at("expsc", [128, 1024], BF16, 94464)
    av_acc = at("av_acc", [64, H * 128], F32, 96512)
    mstd_sb = [at(f"mstd{i}", [128, 128], BF16, 104704 + i * 256) for i in range(2)]
    mtt_sb = [at(f"mtt{i}", [128, 128], BF16, 105216 + i * 256) for i in range(2)]
    qslot = at("qslot", [65, H * 128], BF16, 105728)
    degA_sb = at("degA_sb", [128, 128], BF16, 109824)
    degB_sb = at("degB_sb", [128, 128], BF16, 110080)
    qg1 = at("qg1", [128, 128], BF16, 110336)
    qg2 = at("qg2", [128, 128], BF16, 110592)
    w1_sb = at("w1_sb", [128, 16 * FFH], BF16, 0)
    w2_sb = at("w2_sb", [128, 11 * D], BF16, 45056)
    silu_sb = at("silu_sb", [128, 11 * 1024], BF16, 90112)

    ps_all = ctx.enter_context(nc.psum_tensor("ps_all", [128, 4096], F32))
    psA = ps_all[:, 0:512]
    psB = ps_all[:, 512:1024]
    psV = ps_all[:, 1024:1536]
    psQ = ps_all[:, 1536:2048]
    ps_big = ps_all[:, 0:2048]
    ps_av = ps_all[:, 2048:4096]
    ps1 = [ps_all[:, 0:1024], ps_all[:, 1024:2048]]
    ps2 = ps_all[:, 2048:4096]

    sem_names = ["dx", "dxo", "dwa", "dwf", "dmk", "dtr", "dcs", "dou",
                 "pe", "dve", "act", "pool", "spq"]
    sems = {nm: ctx.enter_context(nc.semaphore(nm)) for nm in sem_names}
    blk = ctx.enter_context(nc.Block())

    RNAMES = (["xstage0", "xstage1", "xnstage0", "xnstage1", "xnfm0", "xnfm1",
               "cs", "kgraw", "kgsw", "kgro", "pxn", "kgp", "bnst0", "bnst1",
               "xn_own", "qslot", "comb", "scol", "consts", "gwt", "gw_sb",
               "cso", "qgbuf", "wks", "wkg", "wv", "wqs", "wqg", "wo", "w1",
               "w2", "silu", "expT0", "expT1", "expsc", "masks", "deg",
               "stats", "route", "mpr", "avac",
               "b0", "b1", "b2", "b3", "b4", "b5", "b6", "b7"]
              + [f"kaug{g}" for g in range(HKV)]
              + [f"v{c}" for c in range(9)]
              + [f"yac{s}" for s in range(NSLOT)])
    BK = {"A": ["b0"], "B": ["b1"], "V": ["b2"], "Q": ["b3"],
          "big": ["b0", "b1", "b2", "b3"], "av": ["b4", "b5", "b6", "b7"],
          "p10": ["b0", "b1"], "p11": ["b2", "b3"], "av0": ["b4"]}

    def program(io: IO):
        t, v, s, g, y = io.t, io.v, io.s, io.g, io.y
        E = io.emit
        R = {nm: Res(nm) for nm in RNAMES}
        bks = lambda key: [R[b] for b in BK[key]]
        KA = [R[f"kaug{gg}"] for gg in range(HKV)]

        E(y, [lambda: y.e.dma_start(out=pswap_sb[:, :], in_=I["pswap"]),
              lambda: y.e.dma_start(out=negid_sb[:, :], in_=I["negid"]),
              lambda: y.e.dma_start(out=lsx_sb[:, :], in_=I["lsx"])],
          writes=[R["consts"]], inc=48, cname="dmk")
        E(g, [lambda: g.e.dma_start(
                out=wks_sb[:, :].rearrange("p (kc m) -> p kc m", m=128),
                in_=I["wks"].rearrange("(kc p) m -> p kc m", p=128))],
          writes=[R["wks"]], inc=16, cname="dwa")
        E(g, [lambda: g.e.dma_start(
                out=wkg_sb[:, :].rearrange("p (kc m) -> p kc m", m=128),
                in_=I["wkg"].rearrange("(kc p) m -> p kc m", p=128))],
          writes=[R["wkg"]], inc=16, cname="dwa")
        E(g, [lambda: g.e.dma_start(
                out=wv_sb[:, :].rearrange("p (kc m) -> p kc m", m=256),
                in_=I["wv"].rearrange("(kc p) m -> p kc m", p=128))],
          writes=[R["wv"]], inc=16, cname="dwa")
        E(v, [lambda: v.e.memset(ones_col[:, :], 1.0)], writes=[R["consts"]])
        E(v, [lambda: v.e.memset(zero_col[:, :], 0.0)], writes=[R["consts"]])
        E(v, [lambda: v.e.memset(epsc, 1.0e-5)],
          writes=[R["bnst0"], R["bnst1"]])
        for gg in range(HKV):
            E(v, [lambda gg=gg: v.e.memset(kaug[gg][64:65, :], 1.0),
                  lambda gg=gg: v.e.memset(kaug[gg][0:64, 48:128], 0.0)],
              writes=[KA[gg]])
        E(v, [lambda: v.e.memset(v_sb[0][:, :], 0.0)], writes=[R["v0"]])
        E(s, [lambda: s.e.activation(scol[:, :], lsx_sb[:, :], AF.Exp)],
          reads=[R["consts"]], writes=[R["scol"]])
        E(v, [lambda: v.e.tensor_scalar(scol[:, :], scol[:, :], SQS, None,
                                        ALU.mult)],
          reads=[R["scol"]], writes=[R["scol"]])

        state = {"tile": 0}

        def ln_tile(src_d, row0, dst, dres, col):
            ti = state["tile"]
            bufi = ti % 2
            xs, xns = xstage[bufi], xnstage[bufi]
            rxs, rxns = R[f"xstage{bufi}"], R[f"xnstage{bufi}"]
            rbn = R[f"bnst{bufi}"]
            dsem = "dx" if src_d is I["xb"] else "dxo"
            E(g, [lambda: g.e.dma_start(out=xs[:, :],
                                        in_=src_d[row0:row0 + 128, :])],
              writes=[rxs], inc=16, cname=dsem)
            for bc in range(4):
                E(v, [lambda bc=bc: v.e.bn_stats(
                        bn6[bufi][:, bc * 6:(bc + 1) * 6],
                        xs[:, bc * 512:(bc + 1) * 512])],
                  reads=[rxs], writes=[rbn])
            E(v, [lambda: v.e.bn_aggr(bn2[bufi], bn6[bufi])],
              reads=[rbn], writes=[rbn])
            E(s, [lambda: s.e.activation(lnv[bufi], bn2_var[bufi], AF.Ln,
                                         bias=epsc)],
              reads=[rbn], writes=[rbn])
            E(s, [lambda: s.e.activation(rstd[bufi], lnv[bufi], AF.Exp,
                                         scale=-0.5)],
              reads=[rbn], writes=[rbn])
            E(g, [lambda: g.e.tensor_scalar(xns[:, :], xs[:, :],
                                            bn2_mean[bufi], rstd[bufi],
                                            ALU.subtract, ALU.mult)],
              reads=[rxs, rbn], writes=[rxns])
            E(y, [lambda: y.e.dma_start(
                    out=dst.rearrange("p (kc t) -> p kc t", kc=16)[:, :, col:col + 128],
                    in_=xns[:, :].rearrange("p (kc t) -> p kc t", t=128),
                    transpose=True)],
              reads=[rxns], writes=[dres], inc=16, cname="dtr")
            state["tile"] += 1

        for hc in range(8):
            base = hc * 512
            xb_fm = xn_fm2[hc % 2]
            rfm = R[f"xnfm{hc % 2}"]
            E(y, [lambda: y.e.dma_start(out=cs_c[:, :],
                                        in_=I["csc"][:, base:base + 512]),
                  lambda: y.e.dma_start(out=cs_s[:, :],
                                        in_=I["css"][:, base:base + 512])],
              writes=[R["cs"]], inc=32, cname="dcs")
            for j in range(4):
                ln_tile(I["xb"], base + j * 128, xb_fm[:, :], rfm, j * 128)
            E(t, [(lambda kc=kc: t.e.matmul(
                    psA[:, :], wkg_sb[:, kc * 128:(kc + 1) * 128],
                    xb_fm[:, kc * 512:(kc + 1) * 512],
                    start=(kc == 0), stop=(kc == 15))) for kc in range(16)],
              reads=[R["wkg"], rfm], writes=bks("A"))
            E(s, [lambda: s.e.activation(kgraw[:, :], psA[:, :], AF.Copy)],
              reads=bks("A"), writes=[R["kgraw"]])
            E(t, [lambda: t.e.matmul(psB[:, :], pswap_sb[:, :], kgraw[:, :],
                                     start=True, stop=True)],
              reads=[R["consts"], R["kgraw"]], writes=bks("B"))
            E(s, [lambda: s.e.activation(kgsw[:, :], psB[:, :], AF.Copy)],
              reads=bks("B"), writes=[R["kgsw"]])
            E(v, [lambda: v.e.tensor_tensor(kgraw[:, :], kgraw[:, :],
                                            cs_c[:, :], ALU.mult)],
              reads=[R["cs"]], writes=[R["kgraw"]])
            E(v, [lambda: v.e.tensor_tensor(kgro[:, :], kgsw[:, :],
                                            cs_s[:, :], ALU.mult)],
              reads=[R["kgsw"], R["cs"]], writes=[R["kgro"]])
            E(v, [lambda: v.e.tensor_tensor(kgro[:, :], kgro[:, :],
                                            kgraw[:, :], ALU.add)],
              reads=[R["kgraw"]], writes=[R["kgro"]])
            if hc < 6:
                E(v, [lambda: v.e.tensor_reduce(
                        kgp[:, hc * 8:(hc + 1) * 8],
                        kgro[:, :].rearrange("p (b w) -> p b w", w=64),
                        AX.X, ALU.add)],
                  reads=[R["kgro"]], writes=[R["kgp"]])
                for kc in range(16):
                    E(v, [lambda kc=kc: v.e.tensor_reduce(
                            scr8[:, :],
                            xb_fm[:, kc * 512:(kc + 1) * 512]
                            .rearrange("p (b w) -> p b w", w=64),
                            AX.X, ALU.add)],
                      reads=[rfm], writes=[R["qgbuf"]])
                    E(s, [lambda kc=kc: s.e.activation(
                            pxn[:, kc * NPOOL + hc * 8: kc * NPOOL + hc * 8 + 8],
                            scr8[:, :], AF.Copy)],
                      reads=[R["qgbuf"]], writes=[R["pxn"]])
            else:
                off = 128 + (base - 3072)
                for gg in range(HKV):
                    E(v, [lambda gg=gg: v.e.tensor_copy(
                            kaug[gg][32:64, off:off + 512],
                            kgro[32 * gg:32 * gg + 32, :])],
                      reads=[R["kgro"]], writes=[KA[gg]])
                E(t, [(lambda kc=kc: t.e.matmul(
                        psB[:, :], wks_sb[:, kc * 128:(kc + 1) * 128],
                        xb_fm[:, kc * 512:(kc + 1) * 512],
                        start=(kc == 0), stop=(kc == 15))) for kc in range(16)],
                  reads=[R["wks"], rfm], writes=bks("B"))
                for gg in range(HKV):
                    E(s, [lambda gg=gg: s.e.activation(
                            kaug[gg][0:32, off:off + 512],
                            psB[32 * gg:32 * gg + 32, :], AF.Copy)],
                      reads=bks("B"), writes=[KA[gg]])
                for j in range(4):
                    tj = (base - 3072) // 128 + j
                    E(t, [(lambda kc=kc, j=j: t.e.matmul(
                            psV[:, 0:256],
                            xb_fm[:, kc * 512 + j * 128: kc * 512 + j * 128 + 128],
                            wv_sb[:, kc * 256:(kc + 1) * 256],
                            start=(kc == 0), stop=(kc == 15)))
                          for kc in range(16)],
                      reads=[R["wv"], rfm], writes=bks("V"))
                    E(s, [lambda tj=tj: s.e.activation(
                            v_sb[1 + tj][:, :], psV[:, 0:256], AF.Copy)],
                      reads=bks("V"), writes=[R[f"v{1 + tj}"]])

        E(t, [(lambda kc=kc: t.e.matmul(
                psA[:, 0:NPOOL], wks_sb[:, kc * 128:(kc + 1) * 128],
                pxn[:, kc * NPOOL:(kc + 1) * NPOOL],
                start=(kc == 0), stop=(kc == 15))) for kc in range(16)],
          reads=[R["wks"], R["pxn"]], writes=bks("A"))
        for gg in range(HKV):
            E(s, [lambda gg=gg: s.e.activation(
                    kaug[gg][0:32, 0:NPOOL],
                    psA[32 * gg:32 * gg + 32, 0:NPOOL], AF.Copy,
                    scale=1.0 / 64.0)],
              reads=bks("A"), writes=[KA[gg]])
        E(t, [(lambda kc=kc: t.e.matmul(
                psV[0:48, 0:256], pxn[:, kc * NPOOL:(kc + 1) * NPOOL],
                wv_sb[:, kc * 256:(kc + 1) * 256],
                start=(kc == 0), stop=(kc == 15))) for kc in range(16)],
          reads=[R["wv"], R["pxn"]], writes=bks("V"))
        E(s, [lambda: s.e.activation(v_sb[0][0:48, :], psV[0:48, 0:256],
                                     AF.Copy, scale=1.0 / 64.0)],
          reads=bks("V"), writes=[R["v0"]])
        for gg in range(HKV):
            E(v, [lambda gg=gg: v.e.tensor_scalar(
                    kaug[gg][32:64, 0:NPOOL], kgp[32 * gg:32 * gg + 32, :],
                    1.0 / 64.0, None, ALU.mult)],
              reads=[R["kgp"]], writes=[KA[gg]])

        E(g, [lambda: g.e.dma_start(out=gw_sb[:, :], in_=I["gw"])],
          writes=[R["gw_sb"], R["wks"]], inc=16, cname="dwa")
        for sl in range(NSLOT):
            E(y, [lambda sl=sl: y.e.dma_start(
                    out=y_acc[:, sl * D:(sl + 1) * D],
                    in_=I["x_own"][sl * 128:(sl + 1) * 128, :])],
              writes=[R[f"yac{sl}"]], inc=16, cname="dxo")
            ln_tile(I["x_own"], sl * 128, xn_own[:, :], R["xn_own"], sl * 128)
        E(t, [(lambda kc=kc: t.e.matmul(
                (psA if kc < 8 else psB)[:, (kc % 8) * 16:(kc % 8) * 16 + 16],
                gw_sb[:, kc * 128:(kc + 1) * 128], negid_sb[0:16, 0:16],
                start=True, stop=True)) for kc in range(16)],
          reads=[R["gw_sb"], R["consts"]], writes=bks("A") + bks("B"))
        E(s, [lambda: s.e.activation(gwt[:, 0:128], psA[:, 0:128], AF.Copy,
                                     scale=-1.0),
              lambda: s.e.activation(gwt[:, 128:256], psB[:, 0:128], AF.Copy,
                                     scale=-1.0)],
          reads=bks("A") + bks("B"), writes=[R["gwt"]])

        E(v, [lambda: v.e.memset(sgate[:, :], 0.0)],
          writes=[R["route"], R["cs"]])
        for sl in range(NSLOT):
            E(t, [(lambda kc=kc, sl=sl: t.e.matmul(
                    psQ[:, 0:16],
                    xn_own[:, kc * 1024 + sl * 128: kc * 1024 + sl * 128 + 128],
                    gwt[:, kc * 16:(kc + 1) * 16],
                    start=(kc == 0), stop=(kc == 15))) for kc in range(16)],
              reads=[R["xn_own"], R["gwt"]], writes=bks("Q"))
            E(s, [lambda: s.e.activation(sgate[:, :], psQ[:, 0:16], AF.Sigmoid)],
              reads=bks("Q"), writes=[R["route"]])
            rt = R["route"]

            def V1(f):
                E(v, [f], reads=[rt], writes=[rt])
            for gg in range(4):
                V1(lambda gg=gg: v.e.tensor_reduce(
                    gm1[:, gg:gg + 1], sgate[:, 4 * gg:4 * gg + 4], AX.X, ALU.max))
            for gg in range(4):
                V1(lambda gg=gg: v.e.tensor_scalar(
                    geq[:, 4 * gg:4 * gg + 4], sgate[:, 4 * gg:4 * gg + 4],
                    gm1[:, gg:gg + 1], None, ALU.is_equal))
                V1(lambda gg=gg: v.e.scalar_tensor_tensor(
                    gtmp[:, 4 * gg:4 * gg + 4], geq[:, 4 * gg:4 * gg + 4], 1.0e30,
                    sgate[:, 4 * gg:4 * gg + 4], ALU.mult, ALU.subtract))
                V1(lambda gg=gg: v.e.tensor_reduce(
                    grm[:, gg:gg + 1], gtmp[:, 4 * gg:4 * gg + 4], AX.X, ALU.min))
            V1(lambda: v.e.tensor_tensor(gs4[:, :], gm1[:, :], grm[:, :],
                                         ALU.subtract))
            V1(lambda: v.e.tensor_reduce(t1c[:, :], gs4[:, :], AX.X, ALU.max))
            V1(lambda: v.e.tensor_scalar(geq[:, 0:4], gs4[:, :], t1c[:, 0:1],
                                         None, ALU.is_equal))
            V1(lambda: v.e.scalar_tensor_tensor(gtmp[:, 0:4], geq[:, 0:4],
                                                1.0e30, gs4[:, :], ALU.mult,
                                                ALU.subtract))
            V1(lambda: v.e.tensor_reduce(rtc[:, :], gtmp[:, 0:4], AX.X, ALU.min))
            V1(lambda: v.e.tensor_scalar(gmask[:, :], gs4[:, :], -1.0,
                                         rtc[:, 0:1], ALU.mult, ALU.is_le))
            V1(lambda: v.e.tensor_copy(
                emask[:, :].rearrange("p (a b) -> p a b", b=4),
                gmask[:, :].rearrange("p (a b) -> p a b", b=1)
                .broadcast_to([128, 4, 4])))
            V1(lambda: v.e.tensor_tensor(gmsk16[:, :], sgate[:, :], emask[:, :],
                                         ALU.mult))
            V1(lambda: v.e.tensor_reduce(t1c[:, :], gmsk16[:, :], AX.X, ALU.max))
            V1(lambda: v.e.tensor_scalar(geq[:, :], gmsk16[:, :], t1c[:, 0:1],
                                         None, ALU.is_equal))
            V1(lambda: v.e.scalar_tensor_tensor(gtmp[:, :], geq[:, :], 1.0e30,
                                                gmsk16[:, :], ALU.mult,
                                                ALU.subtract))
            V1(lambda: v.e.tensor_reduce(rtc[:, :], gtmp[:, :], AX.X, ALU.min))
            V1(lambda: v.e.tensor_scalar(gtmp[:, :], gtmp[:, :], rtc[:, 0:1],
                                         None, ALU.is_equal))
            V1(lambda: v.e.tensor_tensor(geq[:, :], geq[:, :], gtmp[:, :],
                                         ALU.add))
            V1(lambda: v.e.tensor_tensor(gtmp[:, :], gmsk16[:, :], geq[:, :],
                                         ALU.mult))
            V1(lambda: v.e.tensor_tensor(t1c[:, :], t1c[:, :], rtc[:, 0:1],
                                         ALU.subtract))
            V1(lambda: v.e.tensor_scalar(t1c[:, :], t1c[:, :], 1.0e-20, None,
                                         ALU.add))
            V1(lambda: v.e.reciprocal(dinv[:, :], t1c[:, :]))
            E(v, [lambda sl=sl: v.e.tensor_scalar(
                    comb[:, sl * 16:(sl + 1) * 16], gtmp[:, :], dinv[:, 0:1],
                    None, ALU.mult)],
              reads=[rt], writes=[R["comb"]])

        if p3:
            E(g, [lambda: g.e.dma_start(
                    out=wo_sb[:, :].rearrange("p (kc m) -> p kc m", m=D),
                    in_=I["wo"].rearrange("(kc p) m -> p kc m", p=128))],
              writes=[R["wo"], R["gw_sb"], R["gwt"], R["wkg"], R["wv"],
                      R["xnfm0"]], inc=16, cname="dwa")
            E(g, [lambda: g.e.dma_start(
                    out=wqs_sb[:, :].rearrange("p (kc m) -> p kc m", m=512),
                    in_=I["wqs"].rearrange("(kc p) m -> p kc m", p=128))],
              writes=[R["wqs"], R["xnfm1"]], inc=16, cname="dwa")
            E(g, [lambda: g.e.dma_start(
                    out=wqg_sb[:, :].rearrange("p (kc m) -> p kc m", m=512),
                    in_=I["wqg"].rearrange("(kc p) m -> p kc m", p=128))],
              writes=[R["wqg"], R["route"], R["cs"], R["kgraw"], R["kgsw"],
                      R["kgro"], R["pxn"], R["kgp"], R["xstage0"],
                      R["xstage1"], R["xnstage0"]], inc=16, cname="dwa")
            E(y, [lambda: y.e.dma_start(out=cso_c_sb[:, :], in_=I["cso_c"]),
                  lambda: y.e.dma_start(out=cso_s_sb[:, :], in_=I["cso_s"])],
              writes=[R["cso"], R["xnstage0"], R["xnstage1"]], inc=32,
              cname="dcs")
            E(y, [lambda: y.e.dma_start(out=degA_sb[:, :], in_=I["degA"]),
                  lambda: y.e.dma_start(out=degB_sb[:, :], in_=I["degB"])],
              writes=[R["deg"]], inc=32, cname="dmk")
            E(t, [(lambda c=c: t.e.matmul(
                    ps_av[0:1, 0:256], ones_col[:, :], v_sb[c][:, :],
                    start=(c == 1), stop=(c == 8))) for c in range(1, 9)],
              reads=[R["consts"]] + [R[f"v{c}"] for c in range(1, 9)],
              writes=bks("av0"))
            E(s, [lambda: s.e.activation(v_sb[0][64:65, :], ps_av[0:1, 0:256],
                                         AF.Copy, scale=1.0 / 1072.0)],
              reads=bks("av0"), writes=[R["v0"]])

            for sl in range(p3_slots):
                vis = VIS[sl]
                nv = len(vis)
                for cq in range(4):
                    E(t, [(lambda kc=kc, cq=cq, sl=sl: t.e.matmul(
                            psQ[:, 0:128],
                            wqs_sb[:, kc * 512 + cq * 128: kc * 512 + (cq + 1) * 128],
                            xn_own[:, kc * 1024 + sl * 128: kc * 1024 + sl * 128 + 128],
                            start=(kc == 0), stop=(kc == 15))) for kc in range(16)],
                      reads=[R["wqs"], R["xn_own"]], writes=bks("Q"))
                    for hh in range(4):
                        h = 4 * cq + hh
                        E(s, [lambda h=h, hh=hh, cq=cq: s.e.activation(
                                qslot[0:32, h * 128:(h + 1) * 128],
                                psQ[32 * hh:32 * hh + 32, 0:128], AF.Copy,
                                scale=scol[32 * hh:32 * hh + 32, cq:cq + 1])],
                          reads=bks("Q") + [R["scol"]], writes=[R["qslot"]])
                    E(t, [(lambda kc=kc, cq=cq, sl=sl: t.e.matmul(
                            psQ[:, 0:128],
                            wqg_sb[:, kc * 512 + cq * 128: kc * 512 + (cq + 1) * 128],
                            xn_own[:, kc * 1024 + sl * 128: kc * 1024 + sl * 128 + 128],
                            start=(kc == 0), stop=(kc == 15))) for kc in range(16)],
                      reads=[R["wqg"], R["xn_own"]], writes=bks("Q"))
                    E(s, [lambda cq=cq: s.e.activation(
                            qg1[:, :], psQ[:, 0:128], AF.Copy,
                            scale=scol[:, cq:cq + 1])],
                      reads=bks("Q") + [R["scol"]], writes=[R["qgbuf"]])
                    E(t, [lambda: t.e.matmul(psB[:, 0:128], pswap_sb[:, :],
                                             qg1[:, :], start=True, stop=True)],
                      reads=[R["consts"], R["qgbuf"]], writes=bks("B"))
                    E(s, [lambda: s.e.activation(qg2[:, :], psB[:, 0:128],
                                                 AF.Copy)],
                      reads=bks("B"), writes=[R["qgbuf"]])
                    E(v, [lambda sl=sl: v.e.tensor_tensor(
                            qg1[:, :], qg1[:, :],
                            cso_c_sb[:, sl * 128:(sl + 1) * 128], ALU.mult)],
                      reads=[R["cso"]], writes=[R["qgbuf"]])
                    E(v, [lambda sl=sl: v.e.tensor_tensor(
                            qg2[:, :], qg2[:, :],
                            cso_s_sb[:, sl * 128:(sl + 1) * 128], ALU.mult)],
                      reads=[R["cso"]], writes=[R["qgbuf"]])
                    E(v, [lambda: v.e.tensor_tensor(qg1[:, :], qg1[:, :],
                                                    qg2[:, :], ALU.add)],
                      writes=[R["qgbuf"]])
                    for hh in range(4):
                        h = 4 * cq + hh
                        E(v, [lambda h=h, hh=hh: v.e.tensor_copy(
                                qslot[32:64, h * 128:(h + 1) * 128],
                                qg1[32 * hh:32 * hh + 32, :])],
                          reads=[R["qgbuf"]], writes=[R["qslot"]])

                E(v, [lambda: v.e.memset(mcst[:, :], -3.0e38),
                      lambda: v.e.memset(dcst[:, :], 0.0)],
                  writes=[R["stats"]])
                for ci, c in enumerate(vis):
                    mb = mstd_sb[ci % 2]
                    mrk = R["masks"]
                    E(y, [lambda sl=sl, c=c, mb=mb: y.e.dma_start(
                            out=mb[:, :], in_=I["mstd"][sl, c])],
                      writes=[mrk], inc=16, cname="dmk")
                    E(t, [(lambda h=h, c=c: t.e.matmul(
                            ps_big[:, h * 128:(h + 1) * 128],
                            qslot[0:64, h * 128:(h + 1) * 128],
                            kaug[h // 4][0:64, c * 128:(c + 1) * 128],
                            start=True, stop=True)) for h in range(H)],
                      reads=[R["qslot"]] + KA, writes=bks("big"))
                    E(v, [lambda mb=mb: v.e.tensor_tensor(
                            ps_big.rearrange("p (h k) -> p h k", k=128),
                            ps_big.rearrange("p (h k) -> p h k", k=128),
                            mb[:, :].rearrange("p (a k) -> p a k", a=1)
                            .broadcast_to([128, H, 128]), ALU.add)],
                      reads=[mrk], writes=bks("big"))
                    E(v, [lambda c=c: v.e.tensor_reduce(
                            mcst[:, :].rearrange("p (h c) -> p h c", c=9)[:, :, c:c + 1],
                            ps_big.rearrange("p (h k) -> p h k", k=128),
                            AX.X, ALU.max)],
                      reads=bks("big"), writes=[R["stats"]])
                    E(v, [lambda: v.e.tensor_scalar(
                            negm[:, :], mcst[:, :], -1.0, None, ALU.mult)],
                      reads=[R["stats"]], writes=[R["stats"]])
                    for h in range(H):
                        E(s, [lambda h=h, c=c: s.e.activation(
                                expT[0][:, h * 128:(h + 1) * 128],
                                ps_big[:, h * 128:(h + 1) * 128], AF.Exp,
                                bias=negm[:, h * 9 + c:h * 9 + c + 1],
                                accum_out=dcst[:, h * 9 + c:h * 9 + c + 1])],
                          reads=bks("big") + [R["stats"]],
                          writes=[R["expT0"], R["stats"]])

                E(v, [lambda: v.e.tensor_reduce(
                        mall[:, :], mcst[:, :].rearrange("p (h c) -> p h c", c=9),
                        AX.X, ALU.max)], reads=[R["stats"]], writes=[R["stats"]])
                E(v, [lambda: v.e.tensor_tensor(
                        edst[:, :].rearrange("p (h c) -> p h c", c=9),
                        mcst[:, :].rearrange("p (h c) -> p h c", c=9),
                        mall[:, :].rearrange("p (h c) -> p h c", c=1)
                        .broadcast_to([128, H, 9]), ALU.subtract)],
                  reads=[R["stats"]], writes=[R["stats"]])
                E(s, [lambda: s.e.activation(edst[:, :], edst[:, :], AF.Exp)],
                  reads=[R["stats"]], writes=[R["stats"]])
                E(v, [lambda: v.e.tensor_tensor(edst[:, :], edst[:, :],
                                                dcst[:, :], ALU.mult)],
                  reads=[R["stats"]], writes=[R["stats"]])
                E(v, [lambda: v.e.tensor_reduce(
                        dall[:, :], edst[:, :].rearrange("p (h c) -> p h c", c=9),
                        AX.X, ALU.add)], reads=[R["stats"]], writes=[R["stats"]])
                E(s, [lambda: s.e.activation(dall[:, :], dall[:, :], AF.Ln)],
                  reads=[R["stats"]], writes=[R["stats"]])
                E(v, [lambda: v.e.tensor_tensor(mprime[:, :], mall[:, :],
                                                dall[:, :], ALU.add)],
                  reads=[R["stats"]], writes=[R["mpr"]])
                E(v, [lambda: v.e.tensor_copy(mprbf[:, :], mprime[:, :])],
                  reads=[R["mpr"]], writes=[R["mpr"]])
                E(t, [(lambda h=h: t.e.matmul(
                        ps_av[0:1, h * 128:(h + 1) * 128], mprbf[:, h:h + 1],
                        negid_sb[:, :], start=True, stop=True))
                      for h in range(H)],
                  reads=[R["mpr"], R["consts"]], writes=bks("av"))
                E(v, [(lambda h=h: v.e.tensor_copy(
                        qslot[64:65, h * 128:(h + 1) * 128],
                        ps_av[0:1, h * 128:(h + 1) * 128])) for h in range(H)],
                  reads=bks("av"), writes=[R["qslot"]])
                if nv > 1:
                    E(v, [lambda: v.e.memset(av_acc[:, :], 0.0)],
                      writes=[R["avac"]])

                for ci, c in enumerate(vis):
                    eb, ebres = expT[ci % 2], R[f"expT{ci % 2}"]
                    mb = mtt_sb[ci % 2]
                    mrk = R["masks"]
                    E(y, [lambda sl=sl, c=c, mb=mb: y.e.dma_start(
                            out=mb[:, :], in_=I["mtt"][sl, c])],
                      writes=[mrk], inc=16, cname="dmk")
                    E(t, [(lambda h=h, c=c: t.e.matmul(
                            ps_big[:, h * 128:(h + 1) * 128],
                            kaug[h // 4][0:65, c * 128:(c + 1) * 128],
                            qslot[0:65, h * 128:(h + 1) * 128],
                            start=True, stop=True)) for h in range(H)],
                      reads=[R["qslot"]] + KA, writes=bks("big"))
                    E(v, [lambda mb=mb: v.e.tensor_tensor(
                            ps_big.rearrange("p (h k) -> p h k", k=128),
                            ps_big.rearrange("p (h k) -> p h k", k=128),
                            mb[:, :].rearrange("p (a k) -> p a k", a=1)
                            .broadcast_to([128, H, 128]), ALU.add)],
                      reads=[mrk], writes=bks("big"))
                    E(s, [lambda eb=eb: s.e.activation(eb[:, :], ps_big[:, :],
                                                       AF.Exp)],
                      reads=bks("big"), writes=[ebres])
                    if sl == 0 and ci == 0:
                        E(v, [lambda eb=eb: v.e.tensor_tensor(
                                eb[:, :].rearrange("p (h k) -> p h k", k=128),
                                eb[:, :].rearrange("p (h k) -> p h k", k=128),
                                degA_sb[:, :].rearrange("p (a k) -> p a k", a=1)
                                .broadcast_to([128, H, 128]), ALU.mult)],
                          reads=[R["deg"]], writes=[ebres])
                        E(v, [lambda eb=eb: v.e.tensor_tensor(
                                eb[:, :].rearrange("p (h k) -> p h k", k=128),
                                eb[:, :].rearrange("p (h k) -> p h k", k=128),
                                degB_sb[:, :].rearrange("p (a k) -> p a k", a=1)
                                .broadcast_to([128, H, 128]), ALU.add)],
                          reads=[R["deg"]], writes=[ebres])
                    E(t, [(lambda h=h, c=c, eb=eb: t.e.matmul(
                            ps_av[0:64, h * 128:(h + 1) * 128],
                            v_sb[c][:, (h // 4) * 64:(h // 4) * 64 + 64],
                            eb[:, h * 128:(h + 1) * 128],
                            start=True, stop=True)) for h in range(H)],
                      reads=[ebres, R[f"v{c}"]], writes=bks("av"))
                    if nv > 1:
                        E(v, [lambda: v.e.scalar_tensor_tensor(
                                av_acc[:, :], ps_av[0:64, :], 1.0,
                                av_acc[:, :], ALU.mult, ALU.add)],
                          reads=bks("av"), writes=[R["avac"]])

                if nv > 1:
                    E(s, [(lambda h=h: s.e.activation(
                            expsc[(h % 2) * 64:(h % 2) * 64 + 64,
                                  (h // 2) * 128:(h // 2) * 128 + 128],
                            av_acc[0:64, h * 128:(h + 1) * 128], AF.Copy))
                          for h in range(H)],
                      reads=[R["avac"]], writes=[R["expsc"]])
                else:
                    E(s, [(lambda h=h: s.e.activation(
                            expsc[(h % 2) * 64:(h % 2) * 64 + 64,
                                  (h // 2) * 128:(h // 2) * 128 + 128],
                            ps_av[0:64, h * 128:(h + 1) * 128], AF.Copy))
                          for h in range(H)],
                      reads=bks("av"), writes=[R["expsc"]])
                for oc in range(4):
                    E(t, [(lambda hd=hd, oc=oc: t.e.matmul(
                            ps_big[:, oc * 512:(oc + 1) * 512],
                            expsc[:, hd * 128:(hd + 1) * 128],
                            wo_sb[:, hd * D + oc * 512: hd * D + (oc + 1) * 512],
                            start=(hd == 0), stop=(hd == 7))) for hd in range(8)],
                      reads=[R["expsc"], R["wo"]], writes=[R[f"b{oc}"]])
                E(v, [lambda sl=sl: v.e.tensor_tensor(
                        y_acc[:, sl * D:(sl + 1) * D],
                        y_acc[:, sl * D:(sl + 1) * D], ps_big[:, 0:2048],
                        ALU.add)],
                  reads=bks("big"), writes=[R[f"yac{sl}"]])

        if p4:
            experts = [("sw1", "sw2", 0, None)] + \
                      [("w1", "w2", e, e) for e in range(NE)]
            ali1 = [R["wo"], R["wqs"]]
            ali2 = [R["wqs"], R["wqg"], R["cso"], R["route"], R["cs"],
                    R["kgraw"], R["kgsw"], R["kgro"], R["pxn"], R["kgp"],
                    R["xstage0"], R["xstage1"], R["xnstage0"], R["xnstage1"],
                    R["expT0"], R["stats"], R["mpr"]] + KA + \
                   [R[f"v{c}"] for c in range(9)]
            ali3 = [R["expT0"], R["expT1"], R["expsc"], R["avac"], R["masks"],
                    R["stats"], R["mpr"], R["qslot"], R["deg"], R["qgbuf"]]
            for ei, (n1, n2, eidx, ce) in enumerate(experts):
                E(g, [lambda n1=n1, eidx=eidx: g.e.dma_start(
                        out=w1_sb[:, :].rearrange("p (kc h) -> p kc h", h=FFH),
                        in_=I[n1][eidx].rearrange("(kc p) h -> p kc h", p=128))],
                  writes=[R["w1"]] + (ali1 if ei == 0 else []), inc=16,
                  cname="dwf")
                E(g, [lambda n2=n2, eidx=eidx: g.e.dma_start(
                        out=w2_sb[:, :].rearrange("p (hc n) -> p hc n", n=D),
                        in_=I[n2][eidx].rearrange("(hc p) n -> p hc n", p=128))],
                  writes=[R["w2"]] + (ali2 if ei == 0 else []), inc=16,
                  cname="dwf")
                for hcx in range(11):
                    pk = "p10" if hcx % 2 == 0 else "p11"
                    E(t, [(lambda kc=kc, th=th, hcx=hcx: t.e.matmul(
                            ps1[hcx % 2][:, th * 512:(th + 1) * 512],
                            w1_sb[:, kc * FFH + hcx * 128: kc * FFH + (hcx + 1) * 128],
                            xn_own[:, kc * 1024 + th * 512: kc * 1024 + (th + 1) * 512],
                            start=(kc == 0), stop=(kc == 15)))
                          for kc in range(16) for th in range(2)],
                      reads=[R["w1"], R["xn_own"]], writes=bks(pk))
                    E(s, [lambda hcx=hcx: s.e.activation(
                            silu_sb[:, hcx * 1024:(hcx + 1) * 1024],
                            ps1[hcx % 2][:, :], AF.Silu)],
                      reads=bks(pk),
                      writes=[R["silu"]] + (ali3 if ei == 0 and hcx == 0 else []))
                for tc in range(8):
                    E(t, [(lambda hcx=hcx, oc=oc, tc=tc: t.e.matmul(
                            ps2[:, oc * 512:(oc + 1) * 512],
                            silu_sb[:, hcx * 1024 + tc * 128: hcx * 1024 + tc * 128 + 128],
                            w2_sb[:, hcx * D + oc * 512: hcx * D + (oc + 1) * 512],
                            start=(hcx == 0), stop=(hcx == 10)))
                          for hcx in range(11) for oc in range(4)],
                      reads=[R["silu"], R["w2"]], writes=bks("av"))
                    if ce is None:
                        E(v, [lambda tc=tc: v.e.scalar_tensor_tensor(
                                y_acc[:, tc * D:(tc + 1) * D], ps2[:, :], 1.0,
                                y_acc[:, tc * D:(tc + 1) * D], ALU.mult,
                                ALU.add)],
                          reads=bks("av"), writes=[R[f"yac{tc}"]])
                    else:
                        E(v, [lambda tc=tc, ce=ce: v.e.scalar_tensor_tensor(
                                y_acc[:, tc * D:(tc + 1) * D], ps2[:, :],
                                comb[:, tc * 16 + ce:tc * 16 + ce + 1],
                                y_acc[:, tc * D:(tc + 1) * D], ALU.mult,
                                ALU.add)],
                          reads=bks("av") + [R["comb"]], writes=[R[f"yac{tc}"]])

        for sl in range(NSLOT):
            E(y, [lambda sl=sl: y.e.dma_start(
                    out=y_out[sl * 128:(sl + 1) * 128, :],
                    in_=y_acc[:, sl * D:(sl + 1) * D])],
              reads=[R[f"yac{sl}"]], inc=16, cname="dou")
        for nm in dbg:
            tgt = {"xn_own": xn_own, "qslot": qslot, "comb": comb,
                   "kaug0": kaug[0], "kaug3": kaug[3], "v0": v_sb[0],
                   "v8": v_sb[8], "yacc": y_acc}[nm]
            E(y, [lambda tgt=tgt, nm=nm: y.e.dma_start(
                    out=DBGAPS[nm], in_=tgt[:, :])], inc=16, cname="dou")
        io.wait(y, "dou", io.n("dou"))
        return io

    DBGAPS = {}
    for nm in dbg:
        shp = {"xn_own": [128, 16 * 1024], "qslot": [65, H * 128],
               "comb": [128, NSLOT * 16], "kaug0": [65, 9 * 128],
               "kaug3": [65, 9 * 128], "v0": [128, 256], "v8": [128, 256],
               "yacc": [128, NSLOT * D]}[nm]
        dt_ = F32 if nm in ("comb", "yacc") else BF16
        DBGAPS[nm] = nc.dram_tensor("dbg_" + nm, shp, dt_,
                                    kind="ExternalOutput").ap()

    counts = {}
    for nm, eng_attr in [("t", "tensor"), ("v", "vector"), ("s", "scalar"),
                         ("g", "gpsimd"), ("y", "sync")]:
        def body(eng, nm=nm):
            io = IO(sems, nm, eng)
            program(io)
            counts[nm] = dict(io.cnt)
        getattr(blk, eng_attr)(body)
    base = counts["t"]
    for nm in ["v", "s", "g", "y"]:
        assert counts[nm] == base, (nm, counts[nm], base)

    ctx.close()
    return nc


def host_constants(q):
    out = {}
    inv = 1.0 / (10000.0 ** (np.arange(0, 32, 2, dtype=np.float64) / 32.0))
    pos = np.arange(T, dtype=np.float64)
    ang = pos[None, :] * inv[:, None]
    cos = np.cos(ang).astype(np.float32)
    sin = np.sin(ang).astype(np.float32)
    ctab = np.zeros((128, T), np.float32)
    stab = np.zeros((128, T), np.float32)
    for hh in range(4):
        ctab[hh * 32:hh * 32 + 16] = cos
        ctab[hh * 32 + 16:hh * 32 + 32] = cos
        stab[hh * 32:hh * 32 + 16] = -sin
        stab[hh * 32 + 16:hh * 32 + 32] = sin
    out["csc"] = ctab.astype(ml_dtypes.bfloat16)
    out["css"] = stab.astype(ml_dtypes.bfloat16)
    own_tiles = [q + 4 * s for s in range(NSLOT)]
    own_cols = np.concatenate([np.arange(t0 * 128, t0 * 128 + 128)
                               for t0 in own_tiles])
    out["cso_c"] = np.ascontiguousarray(ctab[:, own_cols]).astype(ml_dtypes.bfloat16)
    out["cso_s"] = np.ascontiguousarray(stab[:, own_cols]).astype(ml_dtypes.bfloat16)
    out["negid"] = (-np.eye(128, dtype=np.float32)).astype(ml_dtypes.bfloat16)
    psw = np.zeros((128, 128), np.float32)
    for hh in range(4):
        for dd in range(16):
            psw[hh * 32 + dd, hh * 32 + 16 + dd] = 1.0
            psw[hh * 32 + 16 + dd, hh * 32 + dd] = 1.0
    out["pswap"] = psw.astype(ml_dtypes.bfloat16)

    kpos = np.zeros(9 * 128, np.int64)
    kreal = np.zeros(9 * 128, bool)
    kpos[0:48] = 64 * np.arange(48) + 63
    kreal[0:48] = True
    for j in range(8):
        kpos[128 * (1 + j):128 * (2 + j)] = 3072 + 128 * j + np.arange(128)
        kreal[128 * (1 + j):128 * (2 + j)] = True
    mstd = np.zeros((NSLOT, 9, 128, 128), np.float32)
    mtt = np.zeros((NSLOT, 9, 128, 128), np.float32)
    for s in range(NSLOT):
        t0 = own_tiles[s]
        qpos = t0 * 128 + np.arange(128)
        for c in range(9):
            kp = kpos[c * 128:(c + 1) * 128]
            kr = kreal[c * 128:(c + 1) * 128]
            visible = (kp[None, :] <= qpos[:, None]) & kr[None, :]
            m = np.where(visible, 0.0, NEG).astype(np.float32)
            mstd[s, c] = m
            mtt[s, c] = m.T
    out["mstd"] = mstd.astype(ml_dtypes.bfloat16)
    out["mtt"] = mtt.astype(ml_dtypes.bfloat16)

    degA = np.ones((128, 128), np.float32)
    degB = np.zeros((128, 128), np.float32)
    degA[64, :] = 0.0
    if q == 0:
        degA[:, 0:63] = 0.0
        degB[0:48, 0:63] = 1.0 / 1072.0
        degB[64, 0:63] = 1.0
    out["degA"] = degA.astype(ml_dtypes.bfloat16)
    out["degB"] = degB.astype(ml_dtypes.bfloat16)
    return out, own_cols


def make_in_maps(x, weights, lsx):
    in_maps = []
    own_cols_all = []
    for c in range(8):
        b, q = c // 4, c % 4
        consts, own_cols = host_constants(q)
        own_cols_all.append(own_cols)
        m = dict(weights)
        m["lsx"] = lsx
        m.update(consts)
        m["xb"] = x[b]
        m["x_own"] = np.ascontiguousarray(x[b][own_cols])
        in_maps.append(m)
    return in_maps, own_cols_all


_CACHE = {}


def kernel(L0_attn_x, L0_attn_Wq_sem, L0_attn_Wk_sem, L0_attn_Wq_geo,
           L0_attn_Wk_geo, L0_attn_Wv, L0_attn_Wo, L0_attn_logit_scale,
           L0_ffn_gate_w, L0_ffn_w1, L0_ffn_w2, L0_ffn_sw1, L0_ffn_sw2):
    x = np.ascontiguousarray(np.asarray(L0_attn_x, dtype=np.float32))
    ls = np.asarray(L0_attn_logit_scale, dtype=np.float32)
    lsx = np.zeros((128, 4), np.float32)
    for cq in range(4):
        for hh in range(4):
            lsx[hh * 32:(hh + 1) * 32, cq] = ls[4 * cq + hh]

    if "nc" not in _CACHE:
        _CACHE["nc"] = build_program()
    nc = _CACHE["nc"]

    f32c = lambda a: np.ascontiguousarray(np.asarray(a, np.float32))
    weights = {"wqs": f32c(L0_attn_Wq_sem), "wks": f32c(L0_attn_Wk_sem),
               "wqg": f32c(L0_attn_Wq_geo), "wkg": f32c(L0_attn_Wk_geo),
               "wv": f32c(L0_attn_Wv), "wo": f32c(L0_attn_Wo),
               "gw": f32c(L0_ffn_gate_w), "w1": f32c(L0_ffn_w1),
               "w2": f32c(L0_ffn_w2), "sw1": f32c(L0_ffn_sw1),
               "sw2": f32c(L0_ffn_sw2)}
    in_maps, own_cols_all = make_in_maps(x, weights, lsx)
    res = run_bass_kernel_spmd(nc, in_maps, core_ids=list(range(8)))
    y = np.zeros((B, T, D), np.float32)
    for c in range(8):
        b = c // 4
        y[b][own_cols_all[c]] = res.results[c]["y"]
    return y


# version 03
# speedup vs baseline: 1.4787x; 1.4787x over previous
import numpy as np
import ml_dtypes

import concourse.bass as bass
import concourse.mybir as mybir
from concourse.bass_utils import run_bass_kernel_spmd

F32 = mybir.dt.float32
BF16 = mybir.dt.bfloat16
AF = mybir.ActivationFunctionType
ALU = mybir.AluOpType
AX = mybir.AxisListType

D = 2048
T = 4096
B = 2
H = 16
HKV = 4
NPOOL = 48
NEG = -1.0e9
FFH = 1408
NE = 16
NSLOT = 8
VIS = [[0]] * 6 + [[0, 1, 2, 3, 4], list(range(9))]
SQS = float(1.0 / np.sqrt(32.0))


class _Dummy:
    def then_inc(self, *a, **k):
        return self


_DUM = _Dummy()


class Proxy:
    def __init__(self, eng):
        self.e = eng


class Res:
    __slots__ = ("name", "w", "readers")

    def __init__(self, name):
        self.name = name
        self.w = None
        self.readers = []


class IO:
    def __init__(self, sems, live_name, eng):
        self.cnt = {k: 0 for k in sems}
        self.sems = sems
        self.waited = {k: 0 for k in sems}
        for nm in ["t", "v", "s", "g", "y"]:
            setattr(self, nm, Proxy(eng if nm == live_name else None))
        self.pmap = {}
        self.pmap[id(self.t)] = "pe"
        self.pmap[id(self.v)] = "dve"
        self.pmap[id(self.s)] = "act"
        self.pmap[id(self.g)] = "pool"
        self.pmap[id(self.y)] = "spq"

    def n(self, cname):
        return self.cnt[cname]

    def wait(self, proxy, cname, val):
        if val <= 0:
            return
        if proxy.e is not None:
            if self.waited[cname] >= val:
                return
            proxy.e.wait_ge(self.sems[cname], val)
            self.waited[cname] = val

    def emit(self, proxy, fns, reads=(), writes=(), inc=1, cname=None):
        if cname is None:
            cname = self.pmap[id(proxy)]
        for r in reads:
            if r.w:
                self.wait(proxy, *r.w)
        for r in writes:
            if r.w:
                self.wait(proxy, *r.w)
            for rd in r.readers:
                self.wait(proxy, *rd)
        dma = cname in ("dx", "dxo", "dwa", "dwf", "dmk", "dtr", "dcs", "dou")
        if dma:
            assert inc == 16 * len(fns), (cname, inc, len(fns))
        last = _DUM
        if proxy.e is not None:
            for f in fns:
                last = f()
                if dma and last is not _DUM:
                    last.then_inc(self.sems[cname], 16)
        if (last is not _DUM) and not dma:
            last.then_inc(self.sems[cname], inc)
        self.cnt[cname] += inc
        mark = (cname, self.cnt[cname])
        for r in reads:
            r.readers.append(mark)
        for r in writes:
            r.w = mark
            r.readers = []
        return mark


def build_program(dbg=(), p3=True, p4=True, p3_slots=NSLOT):
    nc = bass.Bass("TRN2", target_bir_lowering=False, debug=False)

    def din(name, shape, dtype=F32):
        return nc.dram_tensor(name, list(shape), dtype, kind="ExternalInput").ap()

    I = {}
    I["xb"] = din("xb", [T, D])
    I["x_own"] = din("x_own", [NSLOT * 128, D])
    I["wqs"] = din("wqs", [D, 512])
    I["wks"] = din("wks", [D, 128])
    I["wqg"] = din("wqg", [D, 512])
    I["wkg"] = din("wkg", [D, 128])
    I["wv"] = din("wv", [D, 256])
    I["wo"] = din("wo", [1024, D])
    I["gw"] = din("gw", [NE, D])
    I["w1"] = din("w1", [NE, D, FFH])
    I["w2"] = din("w2", [NE, FFH, D])
    I["sw1"] = din("sw1", [1, D, FFH])
    I["sw2"] = din("sw2", [1, FFH, D])
    I["lsx"] = din("lsx", [128, 4])
    I["csc"] = din("csc", [128, T], BF16)
    I["css"] = din("css", [128, T], BF16)
    I["cso_c"] = din("cso_c", [128, NSLOT * 128], BF16)
    I["cso_s"] = din("cso_s", [128, NSLOT * 128], BF16)
    I["negid"] = din("negid", [128, 128], BF16)
    I["pswap"] = din("pswap", [128, 128], BF16)
    I["mstd"] = din("mstd", [NSLOT, 9, 128, 128], BF16)
    I["mtt"] = din("mtt", [NSLOT, 9, 128, 128], BF16)
    I["degA"] = din("degA", [128, 128], BF16)
    I["degB"] = din("degB", [128, 128], BF16)
    y_out = nc.dram_tensor("y", [NSLOT * 128, D], F32, kind="ExternalOutput").ap()

    from contextlib import ExitStack
    ctx = ExitStack()
    sball = lambda name, shape, dt: ctx.enter_context(
        nc.sbuf_tensor(name, list(shape), dt))

    y_acc = sball("y_acc", [128, NSLOT * D], F32)
    xn_own = sball("xn_own", [128, 16 * 1024], BF16)
    comb = sball("comb", [128, NSLOT * 16], F32)
    scol = sball("scol", [128, 4], F32)
    ones_col = sball("ones_col", [128, 1], BF16)
    zero_col = sball("zero_col", [128, 1], BF16)
    negid_sb = sball("negid_sb", [128, 128], BF16)
    pswap_sb = sball("pswap_sb", [128, 128], BF16)
    lsx_sb = sball("lsx_sb", [128, 4], F32)
    scr8 = sball("scr8", [128, 8], F32)
    lnscr2 = [sball(f"lnscr{i}", [128, 32], F32) for i in range(2)]
    bn6 = [t[:, 0:24] for t in lnscr2]
    bn2 = [t[:, 24:26] for t in lnscr2]
    bn2_mean = [t[:, 24:25] for t in lnscr2]
    bn2_var = [t[:, 25:26] for t in lnscr2]
    lnv = [t[:, 26:27] for t in lnscr2]
    rstd = [t[:, 27:28] for t in lnscr2]
    epsc = lnscr2[0][:, 29:30]

    ARENA = 112640
    arena = sball("arena", [128, ARENA], mybir.dt.uint8)
    arena_base = nc.lookup_mloc(arena).addr

    def at(name, shape, dtype, off):
        nbytes = int(np.prod(shape[1:])) * mybir.dt.size(dtype)
        assert off + nbytes <= ARENA, f"{name}: {off}+{nbytes} > {ARENA}"
        return nc.alloc_sbuf_tensor_at(name, list(shape), dtype,
                                       offset=arena_base + off)

    wks_sb = at("wks_sb", [128, 16 * 128], BF16, 0)
    wkg_sb = at("wkg_sb", [128, 16 * 128], BF16, 4096)
    wv_sb = at("wv_sb", [128, 16 * 256], BF16, 8192)
    xn_fm2 = [at(f"xn_fm{i}", [128, 16 * 512], BF16, 16384 + 16384 * i)
              for i in range(2)]
    cs_c = at("cs_c", [128, 512], BF16, 49152)
    cs_s = at("cs_s", [128, 512], BF16, 50176)
    kgraw = at("kgraw", [128, 512], BF16, 51200)
    kgsw = at("kgsw", [128, 512], BF16, 52224)
    kgro = at("kgro", [128, 512], BF16, 53248)
    pxn = at("pxn", [128, 16 * NPOOL], BF16, 54272)
    kgp = at("kgp", [128, NPOOL], F32, 55808)
    xstage = [at(f"xst{i}", [128, D], BF16, 56064 + i * 4096) for i in range(2)]
    xnstage = [at(f"xnst{i}", [128, D], BF16, 64256 + i * 4096) for i in range(2)]
    kaug = [at(f"kaug{g}", [65, 9 * 128], BF16, 72448 + g * 2304) for g in range(HKV)]
    v_sb = [at(f"v{c}", [128, 256], BF16, 81664 + c * 512) for c in range(9)]
    gw_sb = at("gw_sb", [16, D], BF16, 0)
    gwt = at("gwt", [128, 256], BF16, 4096)
    sgate = at("sgate", [128, 16], F32, 49152)
    gm1 = at("gm1", [128, 4], F32, 49216)
    grm = at("grm", [128, 4], F32, 49248)
    gs4 = at("gs4", [128, 4], F32, 49280)
    geq = at("geq", [128, 16], F32, 49312)
    gtmp = at("gtmp", [128, 16], F32, 49376)
    gmask = at("gmask", [128, 4], F32, 49440)
    emask = at("emask", [128, 16], F32, 49472)
    gmsk16 = at("gmsk16", [128, 16], F32, 49536)
    t1c = at("t1c", [128, 1], F32, 49600)
    rtc = at("rtc", [128, 1], F32, 49632)
    dinv = at("dinv", [128, 1], F32, 49664)
    wo_sb = at("wo_sb", [128, 8 * D], BF16, 0)
    wqs_sb = at("wqs_sb", [128, 16 * 512], BF16, 32768)
    wqg_sb = at("wqg_sb", [128, 16 * 512], BF16, 49152)
    cso_c_sb = at("cso_c_sb", [128, NSLOT * 128], BF16, 65536)
    cso_s_sb = at("cso_s_sb", [128, NSLOT * 128], BF16, 67584)
    mcst = at("mcst", [128, H * 9], F32, 69632)
    negm = at("negm", [128, H * 9], F32, 70208)
    dcst = at("dcst", [128, H * 9], F32, 70784)
    edst = at("edst", [128, H * 9], F32, 71360)
    mall = at("mall", [128, H], F32, 71936)
    dall = at("dall", [128, H], F32, 72000)
    mprime = at("mprime", [128, H], F32, 72064)
    mprbf = at("mprbf", [128, H], BF16, 72128)
    expT = [at(f"expT{i}", [128, H * 128], BF16, 86272 + i * 4096) for i in range(2)]
    expsc = at("expsc", [128, 1024], BF16, 94464)
    av_acc = at("av_acc", [64, H * 128], F32, 96512)
    mstd_sb = [at(f"mstd{i}", [128, 128], BF16, 104704 + i * 256) for i in range(2)]
    mtt_sb = [at(f"mtt{i}", [128, 128], BF16, 105216 + i * 256) for i in range(2)]
    qslot = at("qslot", [65, H * 128], BF16, 105728)
    degA_sb = at("degA_sb", [128, 128], BF16, 109824)
    degB_sb = at("degB_sb", [128, 128], BF16, 110080)
    qg1 = at("qg1", [128, 128], BF16, 110336)
    qg2 = at("qg2", [128, 128], BF16, 110592)
    w1_sb = at("w1_sb", [128, 16 * FFH], BF16, 0)
    w2_sb = at("w2_sb", [128, 11 * D], BF16, 45056)
    silu_sb = at("silu_sb", [128, 11 * 1024], BF16, 90112)

    ps_all = ctx.enter_context(nc.psum_tensor("ps_all", [128, 4096], F32))
    psA = ps_all[:, 0:512]
    psB = ps_all[:, 512:1024]
    psV = ps_all[:, 1024:1536]
    psQ = ps_all[:, 1536:2048]
    ps_big = ps_all[:, 0:2048]
    ps_av = ps_all[:, 2048:4096]
    ps1 = [ps_all[:, 0:1024], ps_all[:, 1024:2048]]
    ps2 = ps_all[:, 2048:4096]

    sem_names = ["dx", "dxo", "dwa", "dwf", "dmk", "dtr", "dcs", "dou",
                 "pe", "dve", "act", "pool", "spq"]
    sems = {nm: ctx.enter_context(nc.semaphore(nm)) for nm in sem_names}
    blk = ctx.enter_context(nc.Block())

    RNAMES = (["xstage0", "xstage1", "xnstage0", "xnstage1", "xnfm0", "xnfm1",
               "cs", "kgraw", "kgsw", "kgro", "pxn", "kgp", "bnst0", "bnst1",
               "xn_own", "qslot", "comb", "scol", "consts", "gwt", "gw_sb",
               "cso", "qgbuf", "wks", "wkg", "wv", "wqs", "wqg", "wo", "w1",
               "w2", "silu", "expT0", "expT1", "expsc", "masks", "deg",
               "stats", "route", "mpr", "avac",
               "b0", "b1", "b2", "b3", "b4", "b5", "b6", "b7"]
              + [f"kaug{g}" for g in range(HKV)]
              + [f"v{c}" for c in range(9)]
              + [f"yac{s}" for s in range(NSLOT)])
    BK = {"A": ["b0"], "B": ["b1"], "V": ["b2"], "Q": ["b3"],
          "big": ["b0", "b1", "b2", "b3"], "av": ["b4", "b5", "b6", "b7"],
          "p10": ["b0", "b1"], "p11": ["b2", "b3"], "av0": ["b4"]}

    def program(io: IO):
        t, v, s, g, y = io.t, io.v, io.s, io.g, io.y
        E = io.emit
        R = {nm: Res(nm) for nm in RNAMES}
        bks = lambda key: [R[b] for b in BK[key]]
        KA = [R[f"kaug{gg}"] for gg in range(HKV)]

        E(y, [lambda: y.e.dma_start(out=pswap_sb[:, :], in_=I["pswap"]),
              lambda: y.e.dma_start(out=negid_sb[:, :], in_=I["negid"]),
              lambda: y.e.dma_start(out=lsx_sb[:, :], in_=I["lsx"])],
          writes=[R["consts"]], inc=48, cname="dmk")
        E(g, [lambda: g.e.dma_start(
                out=wks_sb[:, :].rearrange("p (kc m) -> p kc m", m=128),
                in_=I["wks"].rearrange("(kc p) m -> p kc m", p=128))],
          writes=[R["wks"]], inc=16, cname="dwa")
        E(g, [lambda: g.e.dma_start(
                out=wkg_sb[:, :].rearrange("p (kc m) -> p kc m", m=128),
                in_=I["wkg"].rearrange("(kc p) m -> p kc m", p=128))],
          writes=[R["wkg"]], inc=16, cname="dwa")
        E(g, [lambda: g.e.dma_start(
                out=wv_sb[:, :].rearrange("p (kc m) -> p kc m", m=256),
                in_=I["wv"].rearrange("(kc p) m -> p kc m", p=128))],
          writes=[R["wv"]], inc=16, cname="dwa")
        E(v, [lambda: v.e.memset(ones_col[:, :], 1.0)], writes=[R["consts"]])
        E(v, [lambda: v.e.memset(zero_col[:, :], 0.0)], writes=[R["consts"]])
        E(v, [lambda: v.e.memset(epsc, 1.0e-5)],
          writes=[R["bnst0"], R["bnst1"]])
        for gg in range(HKV):
            E(v, [lambda gg=gg: v.e.memset(kaug[gg][64:65, :], 1.0),
                  lambda gg=gg: v.e.memset(kaug[gg][0:64, 48:128], 0.0)],
              writes=[KA[gg]])
        E(v, [lambda: v.e.memset(v_sb[0][:, :], 0.0)], writes=[R["v0"]])
        E(s, [lambda: s.e.activation(scol[:, :], lsx_sb[:, :], AF.Exp)],
          reads=[R["consts"]], writes=[R["scol"]])
        E(v, [lambda: v.e.tensor_scalar(scol[:, :], scol[:, :], SQS, None,
                                        ALU.mult)],
          reads=[R["scol"]], writes=[R["scol"]])

        state = {"tile": 0}

        def ln_tile(src_d, row0, dst, dres, col):
            ti = state["tile"]
            bufi = ti % 2
            xs, xns = xstage[bufi], xnstage[bufi]
            rxs, rxns = R[f"xstage{bufi}"], R[f"xnstage{bufi}"]
            rbn = R[f"bnst{bufi}"]
            dsem = "dx" if src_d is I["xb"] else "dxo"
            E(g, [lambda: g.e.dma_start(out=xs[:, :],
                                        in_=src_d[row0:row0 + 128, :])],
              writes=[rxs], inc=16, cname=dsem)
            for bc in range(4):
                E(v, [lambda bc=bc: v.e.bn_stats(
                        bn6[bufi][:, bc * 6:(bc + 1) * 6],
                        xs[:, bc * 512:(bc + 1) * 512])],
                  reads=[rxs], writes=[rbn])
            E(v, [lambda: v.e.bn_aggr(bn2[bufi], bn6[bufi])],
              reads=[rbn], writes=[rbn])
            E(s, [lambda: s.e.activation(lnv[bufi], bn2_var[bufi], AF.Ln,
                                         bias=epsc)],
              reads=[rbn], writes=[rbn])
            E(s, [lambda: s.e.activation(rstd[bufi], lnv[bufi], AF.Exp,
                                         scale=-0.5)],
              reads=[rbn], writes=[rbn])
            E(v, [lambda: v.e.tensor_scalar(xns[:, :], xs[:, :],
                                            bn2_mean[bufi], rstd[bufi],
                                            ALU.subtract, ALU.mult)],
              reads=[rxs, rbn], writes=[rxns])
            E(y, [lambda: y.e.dma_start(
                    out=dst.rearrange("p (kc t) -> p kc t", kc=16)[:, :, col:col + 128],
                    in_=xns[:, :].rearrange("p (kc t) -> p kc t", t=128),
                    transpose=True)],
              reads=[rxns], writes=[dres], inc=16, cname="dtr")
            state["tile"] += 1

        for hc in range(8):
            base = hc * 512
            xb_fm = xn_fm2[hc % 2]
            rfm = R[f"xnfm{hc % 2}"]
            E(y, [lambda: y.e.dma_start(out=cs_c[:, :],
                                        in_=I["csc"][:, base:base + 512]),
                  lambda: y.e.dma_start(out=cs_s[:, :],
                                        in_=I["css"][:, base:base + 512])],
              writes=[R["cs"]], inc=32, cname="dcs")
            for j in range(4):
                ln_tile(I["xb"], base + j * 128, xb_fm[:, :], rfm, j * 128)
            E(t, [(lambda kc=kc: t.e.matmul(
                    psA[:, :], wkg_sb[:, kc * 128:(kc + 1) * 128],
                    xb_fm[:, kc * 512:(kc + 1) * 512],
                    start=(kc == 0), stop=(kc == 15))) for kc in range(16)],
              reads=[R["wkg"], rfm], writes=bks("A"))
            E(s, [lambda: s.e.activation(kgraw[:, :], psA[:, :], AF.Copy)],
              reads=bks("A"), writes=[R["kgraw"]])
            E(t, [lambda: t.e.matmul(psB[:, :], pswap_sb[:, :], kgraw[:, :],
                                     start=True, stop=True)],
              reads=[R["consts"], R["kgraw"]], writes=bks("B"))
            E(s, [lambda: s.e.activation(kgsw[:, :], psB[:, :], AF.Copy)],
              reads=bks("B"), writes=[R["kgsw"]])
            E(v, [lambda: v.e.tensor_tensor(kgraw[:, :], kgraw[:, :],
                                            cs_c[:, :], ALU.mult)],
              reads=[R["cs"]], writes=[R["kgraw"]])
            E(v, [lambda: v.e.tensor_tensor(kgro[:, :], kgsw[:, :],
                                            cs_s[:, :], ALU.mult)],
              reads=[R["kgsw"], R["cs"]], writes=[R["kgro"]])
            E(v, [lambda: v.e.tensor_tensor(kgro[:, :], kgro[:, :],
                                            kgraw[:, :], ALU.add)],
              reads=[R["kgraw"]], writes=[R["kgro"]])
            if hc < 6:
                E(v, [lambda: v.e.tensor_reduce(
                        kgp[:, hc * 8:(hc + 1) * 8],
                        kgro[:, :].rearrange("p (b w) -> p b w", w=64),
                        AX.X, ALU.add)],
                  reads=[R["kgro"]], writes=[R["kgp"]])
                for kc in range(16):
                    E(v, [lambda kc=kc: v.e.tensor_reduce(
                            scr8[:, :],
                            xb_fm[:, kc * 512:(kc + 1) * 512]
                            .rearrange("p (b w) -> p b w", w=64),
                            AX.X, ALU.add)],
                      reads=[rfm], writes=[R["qgbuf"]])
                    E(s, [lambda kc=kc: s.e.activation(
                            pxn[:, kc * NPOOL + hc * 8: kc * NPOOL + hc * 8 + 8],
                            scr8[:, :], AF.Copy)],
                      reads=[R["qgbuf"]], writes=[R["pxn"]])
            else:
                off = 128 + (base - 3072)
                for gg in range(HKV):
                    E(v, [lambda gg=gg: v.e.tensor_copy(
                            kaug[gg][32:64, off:off + 512],
                            kgro[32 * gg:32 * gg + 32, :])],
                      reads=[R["kgro"]], writes=[KA[gg]])
                E(t, [(lambda kc=kc: t.e.matmul(
                        psB[:, :], wks_sb[:, kc * 128:(kc + 1) * 128],
                        xb_fm[:, kc * 512:(kc + 1) * 512],
                        start=(kc == 0), stop=(kc == 15))) for kc in range(16)],
                  reads=[R["wks"], rfm], writes=bks("B"))
                for gg in range(HKV):
                    E(s, [lambda gg=gg: s.e.activation(
                            kaug[gg][0:32, off:off + 512],
                            psB[32 * gg:32 * gg + 32, :], AF.Copy)],
                      reads=bks("B"), writes=[KA[gg]])
                for j in range(4):
                    tj = (base - 3072) // 128 + j
                    E(t, [(lambda kc=kc, j=j: t.e.matmul(
                            psV[:, 0:256],
                            xb_fm[:, kc * 512 + j * 128: kc * 512 + j * 128 + 128],
                            wv_sb[:, kc * 256:(kc + 1) * 256],
                            start=(kc == 0), stop=(kc == 15)))
                          for kc in range(16)],
                      reads=[R["wv"], rfm], writes=bks("V"))
                    E(s, [lambda tj=tj: s.e.activation(
                            v_sb[1 + tj][:, :], psV[:, 0:256], AF.Copy)],
                      reads=bks("V"), writes=[R[f"v{1 + tj}"]])

        E(t, [(lambda kc=kc: t.e.matmul(
                psA[:, 0:NPOOL], wks_sb[:, kc * 128:(kc + 1) * 128],
                pxn[:, kc * NPOOL:(kc + 1) * NPOOL],
                start=(kc == 0), stop=(kc == 15))) for kc in range(16)],
          reads=[R["wks"], R["pxn"]], writes=bks("A"))
        for gg in range(HKV):
            E(s, [lambda gg=gg: s.e.activation(
                    kaug[gg][0:32, 0:NPOOL],
                    psA[32 * gg:32 * gg + 32, 0:NPOOL], AF.Copy,
                    scale=1.0 / 64.0)],
              reads=bks("A"), writes=[KA[gg]])
        E(t, [(lambda kc=kc: t.e.matmul(
                psV[0:48, 0:256], pxn[:, kc * NPOOL:(kc + 1) * NPOOL],
                wv_sb[:, kc * 256:(kc + 1) * 256],
                start=(kc == 0), stop=(kc == 15))) for kc in range(16)],
          reads=[R["wv"], R["pxn"]], writes=bks("V"))
        E(s, [lambda: s.e.activation(v_sb[0][0:48, :], psV[0:48, 0:256],
                                     AF.Copy, scale=1.0 / 64.0)],
          reads=bks("V"), writes=[R["v0"]])
        for gg in range(HKV):
            E(v, [lambda gg=gg: v.e.tensor_scalar(
                    kaug[gg][32:64, 0:NPOOL], kgp[32 * gg:32 * gg + 32, :],
                    1.0 / 64.0, None, ALU.mult)],
              reads=[R["kgp"]], writes=[KA[gg]])

        E(g, [lambda: g.e.dma_start(out=gw_sb[:, :], in_=I["gw"])],
          writes=[R["gw_sb"], R["wks"]], inc=16, cname="dwa")
        for sl in range(NSLOT):
            E(y, [lambda sl=sl: y.e.dma_start(
                    out=y_acc[:, sl * D:(sl + 1) * D],
                    in_=I["x_own"][sl * 128:(sl + 1) * 128, :])],
              writes=[R[f"yac{sl}"]], inc=16, cname="dxo")
            ln_tile(I["x_own"], sl * 128, xn_own[:, :], R["xn_own"], sl * 128)
        E(t, [(lambda kc=kc: t.e.matmul(
                (psA if kc < 8 else psB)[:, (kc % 8) * 16:(kc % 8) * 16 + 16],
                gw_sb[:, kc * 128:(kc + 1) * 128], negid_sb[0:16, 0:16],
                start=True, stop=True)) for kc in range(16)],
          reads=[R["gw_sb"], R["consts"]], writes=bks("A") + bks("B"))
        E(s, [lambda: s.e.activation(gwt[:, 0:128], psA[:, 0:128], AF.Copy,
                                     scale=-1.0),
              lambda: s.e.activation(gwt[:, 128:256], psB[:, 0:128], AF.Copy,
                                     scale=-1.0)],
          reads=bks("A") + bks("B"), writes=[R["gwt"]])

        E(v, [lambda: v.e.memset(sgate[:, :], 0.0)],
          writes=[R["route"], R["cs"]])
        for sl in range(NSLOT):
            E(t, [(lambda kc=kc, sl=sl: t.e.matmul(
                    psQ[:, 0:16],
                    xn_own[:, kc * 1024 + sl * 128: kc * 1024 + sl * 128 + 128],
                    gwt[:, kc * 16:(kc + 1) * 16],
                    start=(kc == 0), stop=(kc == 15))) for kc in range(16)],
              reads=[R["xn_own"], R["gwt"]], writes=bks("Q"))
            E(s, [lambda: s.e.activation(sgate[:, :], psQ[:, 0:16], AF.Sigmoid)],
              reads=bks("Q"), writes=[R["route"]])
            rt = R["route"]

            def V1(f):
                E(v, [f], reads=[rt], writes=[rt])
            for gg in range(4):
                V1(lambda gg=gg: v.e.tensor_reduce(
                    gm1[:, gg:gg + 1], sgate[:, 4 * gg:4 * gg + 4], AX.X, ALU.max))
            for gg in range(4):
                V1(lambda gg=gg: v.e.tensor_scalar(
                    geq[:, 4 * gg:4 * gg + 4], sgate[:, 4 * gg:4 * gg + 4],
                    gm1[:, gg:gg + 1], None, ALU.is_equal))
                V1(lambda gg=gg: v.e.scalar_tensor_tensor(
                    gtmp[:, 4 * gg:4 * gg + 4], geq[:, 4 * gg:4 * gg + 4], 1.0e30,
                    sgate[:, 4 * gg:4 * gg + 4], ALU.mult, ALU.subtract))
                V1(lambda gg=gg: v.e.tensor_reduce(
                    grm[:, gg:gg + 1], gtmp[:, 4 * gg:4 * gg + 4], AX.X, ALU.min))
            V1(lambda: v.e.tensor_tensor(gs4[:, :], gm1[:, :], grm[:, :],
                                         ALU.subtract))
            V1(lambda: v.e.tensor_reduce(t1c[:, :], gs4[:, :], AX.X, ALU.max))
            V1(lambda: v.e.tensor_scalar(geq[:, 0:4], gs4[:, :], t1c[:, 0:1],
                                         None, ALU.is_equal))
            V1(lambda: v.e.scalar_tensor_tensor(gtmp[:, 0:4], geq[:, 0:4],
                                                1.0e30, gs4[:, :], ALU.mult,
                                                ALU.subtract))
            V1(lambda: v.e.tensor_reduce(rtc[:, :], gtmp[:, 0:4], AX.X, ALU.min))
            V1(lambda: v.e.tensor_scalar(gmask[:, :], gs4[:, :], -1.0,
                                         rtc[:, 0:1], ALU.mult, ALU.is_le))
            V1(lambda: v.e.tensor_copy(
                emask[:, :].rearrange("p (a b) -> p a b", b=4),
                gmask[:, :].rearrange("p (a b) -> p a b", b=1)
                .broadcast_to([128, 4, 4])))
            V1(lambda: v.e.tensor_tensor(gmsk16[:, :], sgate[:, :], emask[:, :],
                                         ALU.mult))
            V1(lambda: v.e.tensor_reduce(t1c[:, :], gmsk16[:, :], AX.X, ALU.max))
            V1(lambda: v.e.tensor_scalar(geq[:, :], gmsk16[:, :], t1c[:, 0:1],
                                         None, ALU.is_equal))
            V1(lambda: v.e.scalar_tensor_tensor(gtmp[:, :], geq[:, :], 1.0e30,
                                                gmsk16[:, :], ALU.mult,
                                                ALU.subtract))
            V1(lambda: v.e.tensor_reduce(rtc[:, :], gtmp[:, :], AX.X, ALU.min))
            V1(lambda: v.e.tensor_scalar(gtmp[:, :], gtmp[:, :], rtc[:, 0:1],
                                         None, ALU.is_equal))
            V1(lambda: v.e.tensor_tensor(geq[:, :], geq[:, :], gtmp[:, :],
                                         ALU.add))
            V1(lambda: v.e.tensor_tensor(gtmp[:, :], gmsk16[:, :], geq[:, :],
                                         ALU.mult))
            V1(lambda: v.e.tensor_tensor(t1c[:, :], t1c[:, :], rtc[:, 0:1],
                                         ALU.subtract))
            V1(lambda: v.e.tensor_scalar(t1c[:, :], t1c[:, :], 1.0e-20, None,
                                         ALU.add))
            V1(lambda: v.e.reciprocal(dinv[:, :], t1c[:, :]))
            E(v, [lambda sl=sl: v.e.tensor_scalar(
                    comb[:, sl * 16:(sl + 1) * 16], gtmp[:, :], dinv[:, 0:1],
                    None, ALU.mult)],
              reads=[rt], writes=[R["comb"]])

        if p3:
            E(g, [lambda: g.e.dma_start(
                    out=wo_sb[:, :].rearrange("p (kc m) -> p kc m", m=D),
                    in_=I["wo"].rearrange("(kc p) m -> p kc m", p=128))],
              writes=[R["wo"], R["gw_sb"], R["gwt"], R["wkg"], R["wv"],
                      R["xnfm0"]], inc=16, cname="dwa")
            E(g, [lambda: g.e.dma_start(
                    out=wqs_sb[:, :].rearrange("p (kc m) -> p kc m", m=512),
                    in_=I["wqs"].rearrange("(kc p) m -> p kc m", p=128))],
              writes=[R["wqs"], R["xnfm1"]], inc=16, cname="dwa")
            E(g, [lambda: g.e.dma_start(
                    out=wqg_sb[:, :].rearrange("p (kc m) -> p kc m", m=512),
                    in_=I["wqg"].rearrange("(kc p) m -> p kc m", p=128))],
              writes=[R["wqg"], R["route"], R["cs"], R["kgraw"], R["kgsw"],
                      R["kgro"], R["pxn"], R["kgp"], R["xstage0"],
                      R["xstage1"], R["xnstage0"]], inc=16, cname="dwa")
            E(y, [lambda: y.e.dma_start(out=cso_c_sb[:, :], in_=I["cso_c"]),
                  lambda: y.e.dma_start(out=cso_s_sb[:, :], in_=I["cso_s"])],
              writes=[R["cso"], R["xnstage0"], R["xnstage1"]], inc=32,
              cname="dcs")
            E(y, [lambda: y.e.dma_start(out=degA_sb[:, :], in_=I["degA"]),
                  lambda: y.e.dma_start(out=degB_sb[:, :], in_=I["degB"])],
              writes=[R["deg"]], inc=32, cname="dmk")
            E(t, [(lambda c=c: t.e.matmul(
                    ps_av[0:1, 0:256], ones_col[:, :], v_sb[c][:, :],
                    start=(c == 1), stop=(c == 8))) for c in range(1, 9)],
              reads=[R["consts"]] + [R[f"v{c}"] for c in range(1, 9)],
              writes=bks("av0"))
            E(s, [lambda: s.e.activation(v_sb[0][64:65, :], ps_av[0:1, 0:256],
                                         AF.Copy, scale=1.0 / 1072.0)],
              reads=bks("av0"), writes=[R["v0"]])

            for sl in range(p3_slots):
                vis = VIS[sl]
                nv = len(vis)
                for cq in range(4):
                    E(t, [(lambda kc=kc, cq=cq, sl=sl: t.e.matmul(
                            psQ[:, 0:128],
                            wqs_sb[:, kc * 512 + cq * 128: kc * 512 + (cq + 1) * 128],
                            xn_own[:, kc * 1024 + sl * 128: kc * 1024 + sl * 128 + 128],
                            start=(kc == 0), stop=(kc == 15))) for kc in range(16)],
                      reads=[R["wqs"], R["xn_own"]], writes=bks("Q"))
                    for hh in range(4):
                        h = 4 * cq + hh
                        E(s, [lambda h=h, hh=hh, cq=cq: s.e.activation(
                                qslot[0:32, h * 128:(h + 1) * 128],
                                psQ[32 * hh:32 * hh + 32, 0:128], AF.Copy,
                                scale=scol[32 * hh:32 * hh + 32, cq:cq + 1])],
                          reads=bks("Q") + [R["scol"]], writes=[R["qslot"]])
                    E(t, [(lambda kc=kc, cq=cq, sl=sl: t.e.matmul(
                            psQ[:, 0:128],
                            wqg_sb[:, kc * 512 + cq * 128: kc * 512 + (cq + 1) * 128],
                            xn_own[:, kc * 1024 + sl * 128: kc * 1024 + sl * 128 + 128],
                            start=(kc == 0), stop=(kc == 15))) for kc in range(16)],
                      reads=[R["wqg"], R["xn_own"]], writes=bks("Q"))
                    E(s, [lambda cq=cq: s.e.activation(
                            qg1[:, :], psQ[:, 0:128], AF.Copy,
                            scale=scol[:, cq:cq + 1])],
                      reads=bks("Q") + [R["scol"]], writes=[R["qgbuf"]])
                    E(t, [lambda: t.e.matmul(psB[:, 0:128], pswap_sb[:, :],
                                             qg1[:, :], start=True, stop=True)],
                      reads=[R["consts"], R["qgbuf"]], writes=bks("B"))
                    E(s, [lambda: s.e.activation(qg2[:, :], psB[:, 0:128],
                                                 AF.Copy)],
                      reads=bks("B"), writes=[R["qgbuf"]])
                    E(v, [lambda sl=sl: v.e.tensor_tensor(
                            qg1[:, :], qg1[:, :],
                            cso_c_sb[:, sl * 128:(sl + 1) * 128], ALU.mult)],
                      reads=[R["cso"]], writes=[R["qgbuf"]])
                    E(v, [lambda sl=sl: v.e.tensor_tensor(
                            qg2[:, :], qg2[:, :],
                            cso_s_sb[:, sl * 128:(sl + 1) * 128], ALU.mult)],
                      reads=[R["cso"]], writes=[R["qgbuf"]])
                    E(v, [lambda: v.e.tensor_tensor(qg1[:, :], qg1[:, :],
                                                    qg2[:, :], ALU.add)],
                      writes=[R["qgbuf"]])
                    for hh in range(4):
                        h = 4 * cq + hh
                        E(v, [lambda h=h, hh=hh: v.e.tensor_copy(
                                qslot[32:64, h * 128:(h + 1) * 128],
                                qg1[32 * hh:32 * hh + 32, :])],
                          reads=[R["qgbuf"]], writes=[R["qslot"]])

                E(v, [lambda: v.e.memset(mcst[:, :], -3.0e38),
                      lambda: v.e.memset(dcst[:, :], 0.0)],
                  writes=[R["stats"]])
                for ci, c in enumerate(vis):
                    mb = mstd_sb[ci % 2]
                    mrk = R["masks"]
                    E(y, [lambda sl=sl, c=c, mb=mb: y.e.dma_start(
                            out=mb[:, :], in_=I["mstd"][sl, c])],
                      writes=[mrk], inc=16, cname="dmk")
                    E(t, [(lambda h=h, c=c: t.e.matmul(
                            ps_big[:, h * 128:(h + 1) * 128],
                            qslot[0:64, h * 128:(h + 1) * 128],
                            kaug[h // 4][0:64, c * 128:(c + 1) * 128],
                            start=True, stop=True)) for h in range(H)],
                      reads=[R["qslot"]] + KA, writes=bks("big"))
                    E(v, [lambda mb=mb: v.e.tensor_tensor(
                            ps_big.rearrange("p (h k) -> p h k", k=128),
                            ps_big.rearrange("p (h k) -> p h k", k=128),
                            mb[:, :].rearrange("p (a k) -> p a k", a=1)
                            .broadcast_to([128, H, 128]), ALU.add)],
                      reads=[mrk], writes=bks("big"))
                    E(v, [lambda c=c: v.e.tensor_reduce(
                            mcst[:, :].rearrange("p (h c) -> p h c", c=9)[:, :, c:c + 1],
                            ps_big.rearrange("p (h k) -> p h k", k=128),
                            AX.X, ALU.max)],
                      reads=bks("big"), writes=[R["stats"]])
                    E(v, [lambda: v.e.tensor_scalar(
                            negm[:, :], mcst[:, :], -1.0, None, ALU.mult)],
                      reads=[R["stats"]], writes=[R["stats"]])
                    for h in range(H):
                        E(s, [lambda h=h, c=c: s.e.activation(
                                expT[0][:, h * 128:(h + 1) * 128],
                                ps_big[:, h * 128:(h + 1) * 128], AF.Exp,
                                bias=negm[:, h * 9 + c:h * 9 + c + 1],
                                accum_out=dcst[:, h * 9 + c:h * 9 + c + 1])],
                          reads=bks("big") + [R["stats"]],
                          writes=[R["expT0"], R["stats"]])

                E(v, [lambda: v.e.tensor_reduce(
                        mall[:, :], mcst[:, :].rearrange("p (h c) -> p h c", c=9),
                        AX.X, ALU.max)], reads=[R["stats"]], writes=[R["stats"]])
                E(v, [lambda: v.e.tensor_tensor(
                        edst[:, :].rearrange("p (h c) -> p h c", c=9),
                        mcst[:, :].rearrange("p (h c) -> p h c", c=9),
                        mall[:, :].rearrange("p (h c) -> p h c", c=1)
                        .broadcast_to([128, H, 9]), ALU.subtract)],
                  reads=[R["stats"]], writes=[R["stats"]])
                E(s, [lambda: s.e.activation(edst[:, :], edst[:, :], AF.Exp)],
                  reads=[R["stats"]], writes=[R["stats"]])
                E(v, [lambda: v.e.tensor_tensor(edst[:, :], edst[:, :],
                                                dcst[:, :], ALU.mult)],
                  reads=[R["stats"]], writes=[R["stats"]])
                E(v, [lambda: v.e.tensor_reduce(
                        dall[:, :], edst[:, :].rearrange("p (h c) -> p h c", c=9),
                        AX.X, ALU.add)], reads=[R["stats"]], writes=[R["stats"]])
                E(s, [lambda: s.e.activation(dall[:, :], dall[:, :], AF.Ln)],
                  reads=[R["stats"]], writes=[R["stats"]])
                E(v, [lambda: v.e.tensor_tensor(mprime[:, :], mall[:, :],
                                                dall[:, :], ALU.add)],
                  reads=[R["stats"]], writes=[R["mpr"]])
                E(v, [lambda: v.e.tensor_copy(mprbf[:, :], mprime[:, :])],
                  reads=[R["mpr"]], writes=[R["mpr"]])
                E(t, [(lambda h=h: t.e.matmul(
                        ps_av[0:1, h * 128:(h + 1) * 128], mprbf[:, h:h + 1],
                        negid_sb[:, :], start=True, stop=True))
                      for h in range(H)],
                  reads=[R["mpr"], R["consts"]], writes=bks("av"))
                E(v, [(lambda h=h: v.e.tensor_copy(
                        qslot[64:65, h * 128:(h + 1) * 128],
                        ps_av[0:1, h * 128:(h + 1) * 128])) for h in range(H)],
                  reads=bks("av"), writes=[R["qslot"]])
                if nv > 1:
                    E(v, [lambda: v.e.memset(av_acc[:, :], 0.0)],
                      writes=[R["avac"]])

                for ci, c in enumerate(vis):
                    eb, ebres = expT[ci % 2], R[f"expT{ci % 2}"]
                    mb = mtt_sb[ci % 2]
                    mrk = R["masks"]
                    E(y, [lambda sl=sl, c=c, mb=mb: y.e.dma_start(
                            out=mb[:, :], in_=I["mtt"][sl, c])],
                      writes=[mrk], inc=16, cname="dmk")
                    E(t, [(lambda h=h, c=c: t.e.matmul(
                            ps_big[:, h * 128:(h + 1) * 128],
                            kaug[h // 4][0:65, c * 128:(c + 1) * 128],
                            qslot[0:65, h * 128:(h + 1) * 128],
                            start=True, stop=True)) for h in range(H)],
                      reads=[R["qslot"]] + KA, writes=bks("big"))
                    E(v, [lambda mb=mb: v.e.tensor_tensor(
                            ps_big.rearrange("p (h k) -> p h k", k=128),
                            ps_big.rearrange("p (h k) -> p h k", k=128),
                            mb[:, :].rearrange("p (a k) -> p a k", a=1)
                            .broadcast_to([128, H, 128]), ALU.add)],
                      reads=[mrk], writes=bks("big"))
                    E(s, [lambda eb=eb: s.e.activation(eb[:, :], ps_big[:, :],
                                                       AF.Exp)],
                      reads=bks("big"), writes=[ebres])
                    if sl == 0 and ci == 0:
                        E(v, [lambda eb=eb: v.e.tensor_tensor(
                                eb[:, :].rearrange("p (h k) -> p h k", k=128),
                                eb[:, :].rearrange("p (h k) -> p h k", k=128),
                                degA_sb[:, :].rearrange("p (a k) -> p a k", a=1)
                                .broadcast_to([128, H, 128]), ALU.mult)],
                          reads=[R["deg"]], writes=[ebres])
                        E(v, [lambda eb=eb: v.e.tensor_tensor(
                                eb[:, :].rearrange("p (h k) -> p h k", k=128),
                                eb[:, :].rearrange("p (h k) -> p h k", k=128),
                                degB_sb[:, :].rearrange("p (a k) -> p a k", a=1)
                                .broadcast_to([128, H, 128]), ALU.add)],
                          reads=[R["deg"]], writes=[ebres])
                    E(t, [(lambda h=h, c=c, eb=eb: t.e.matmul(
                            ps_av[0:64, h * 128:(h + 1) * 128],
                            v_sb[c][:, (h // 4) * 64:(h // 4) * 64 + 64],
                            eb[:, h * 128:(h + 1) * 128],
                            start=True, stop=True)) for h in range(H)],
                      reads=[ebres, R[f"v{c}"]], writes=bks("av"))
                    if nv > 1:
                        E(v, [lambda: v.e.scalar_tensor_tensor(
                                av_acc[:, :], ps_av[0:64, :], 1.0,
                                av_acc[:, :], ALU.mult, ALU.add)],
                          reads=bks("av"), writes=[R["avac"]])

                if nv > 1:
                    E(s, [(lambda h=h: s.e.activation(
                            expsc[(h % 2) * 64:(h % 2) * 64 + 64,
                                  (h // 2) * 128:(h // 2) * 128 + 128],
                            av_acc[0:64, h * 128:(h + 1) * 128], AF.Copy))
                          for h in range(H)],
                      reads=[R["avac"]], writes=[R["expsc"]])
                else:
                    E(s, [(lambda h=h: s.e.activation(
                            expsc[(h % 2) * 64:(h % 2) * 64 + 64,
                                  (h // 2) * 128:(h // 2) * 128 + 128],
                            ps_av[0:64, h * 128:(h + 1) * 128], AF.Copy))
                          for h in range(H)],
                      reads=bks("av"), writes=[R["expsc"]])
                for oc in range(4):
                    E(t, [(lambda hd=hd, oc=oc: t.e.matmul(
                            ps_big[:, oc * 512:(oc + 1) * 512],
                            expsc[:, hd * 128:(hd + 1) * 128],
                            wo_sb[:, hd * D + oc * 512: hd * D + (oc + 1) * 512],
                            start=(hd == 0), stop=(hd == 7))) for hd in range(8)],
                      reads=[R["expsc"], R["wo"]], writes=[R[f"b{oc}"]])
                E(v, [lambda sl=sl: v.e.tensor_tensor(
                        y_acc[:, sl * D:(sl + 1) * D],
                        y_acc[:, sl * D:(sl + 1) * D], ps_big[:, 0:2048],
                        ALU.add)],
                  reads=bks("big"), writes=[R[f"yac{sl}"]])

        if p4:
            experts = [("sw1", "sw2", 0, None)] + \
                      [("w1", "w2", e, e) for e in range(NE)]
            ali1 = [R["wo"], R["wqs"]]
            ali2 = [R["wqs"], R["wqg"], R["cso"], R["route"], R["cs"],
                    R["kgraw"], R["kgsw"], R["kgro"], R["pxn"], R["kgp"],
                    R["xstage0"], R["xstage1"], R["xnstage0"], R["xnstage1"],
                    R["expT0"], R["stats"], R["mpr"]] + KA + \
                   [R[f"v{c}"] for c in range(9)]
            ali3 = [R["expT0"], R["expT1"], R["expsc"], R["avac"], R["masks"],
                    R["stats"], R["mpr"], R["qslot"], R["deg"], R["qgbuf"]]
            for ei, (n1, n2, eidx, ce) in enumerate(experts):
                E(g, [lambda n1=n1, eidx=eidx: g.e.dma_start(
                        out=w1_sb[:, :].rearrange("p (kc h) -> p kc h", h=FFH),
                        in_=I[n1][eidx].rearrange("(kc p) h -> p kc h", p=128))],
                  writes=[R["w1"]] + (ali1 if ei == 0 else []), inc=16,
                  cname="dwf")
                E(g, [lambda n2=n2, eidx=eidx: g.e.dma_start(
                        out=w2_sb[:, :].rearrange("p (hc n) -> p hc n", n=D),
                        in_=I[n2][eidx].rearrange("(hc p) n -> p hc n", p=128))],
                  writes=[R["w2"]] + (ali2 if ei == 0 else []), inc=16,
                  cname="dwf")
                for hcx in range(11):
                    pk = "p10" if hcx % 2 == 0 else "p11"
                    E(t, [(lambda kc=kc, th=th, hcx=hcx: t.e.matmul(
                            ps1[hcx % 2][:, th * 512:(th + 1) * 512],
                            w1_sb[:, kc * FFH + hcx * 128: kc * FFH + (hcx + 1) * 128],
                            xn_own[:, kc * 1024 + th * 512: kc * 1024 + (th + 1) * 512],
                            start=(kc == 0), stop=(kc == 15)))
                          for kc in range(16) for th in range(2)],
                      reads=[R["w1"], R["xn_own"]], writes=bks(pk))
                    E(s, [lambda hcx=hcx: s.e.activation(
                            silu_sb[:, hcx * 1024:(hcx + 1) * 1024],
                            ps1[hcx % 2][:, :], AF.Silu)],
                      reads=bks(pk),
                      writes=[R["silu"]] + (ali3 if ei == 0 and hcx == 0 else []))
                for tc in range(8):
                    E(t, [(lambda hcx=hcx, oc=oc, tc=tc: t.e.matmul(
                            ps2[:, oc * 512:(oc + 1) * 512],
                            silu_sb[:, hcx * 1024 + tc * 128: hcx * 1024 + tc * 128 + 128],
                            w2_sb[:, hcx * D + oc * 512: hcx * D + (oc + 1) * 512],
                            start=(hcx == 0), stop=(hcx == 10)))
                          for hcx in range(11) for oc in range(4)],
                      reads=[R["silu"], R["w2"]], writes=bks("av"))
                    if ce is None:
                        E(v, [lambda tc=tc: v.e.scalar_tensor_tensor(
                                y_acc[:, tc * D:(tc + 1) * D], ps2[:, :], 1.0,
                                y_acc[:, tc * D:(tc + 1) * D], ALU.mult,
                                ALU.add)],
                          reads=bks("av"), writes=[R[f"yac{tc}"]])
                    else:
                        E(v, [lambda tc=tc, ce=ce: v.e.scalar_tensor_tensor(
                                y_acc[:, tc * D:(tc + 1) * D], ps2[:, :],
                                comb[:, tc * 16 + ce:tc * 16 + ce + 1],
                                y_acc[:, tc * D:(tc + 1) * D], ALU.mult,
                                ALU.add)],
                          reads=bks("av") + [R["comb"]], writes=[R[f"yac{tc}"]])

        for sl in range(NSLOT):
            E(y, [lambda sl=sl: y.e.dma_start(
                    out=y_out[sl * 128:(sl + 1) * 128, :],
                    in_=y_acc[:, sl * D:(sl + 1) * D])],
              reads=[R[f"yac{sl}"]], inc=16, cname="dou")
        for nm in dbg:
            tgt = {"xn_own": xn_own, "qslot": qslot, "comb": comb,
                   "kaug0": kaug[0], "kaug3": kaug[3], "v0": v_sb[0],
                   "v8": v_sb[8], "yacc": y_acc}[nm]
            E(y, [lambda tgt=tgt, nm=nm: y.e.dma_start(
                    out=DBGAPS[nm], in_=tgt[:, :])], inc=16, cname="dou")
        io.wait(y, "dou", io.n("dou"))
        return io

    DBGAPS = {}
    for nm in dbg:
        shp = {"xn_own": [128, 16 * 1024], "qslot": [65, H * 128],
               "comb": [128, NSLOT * 16], "kaug0": [65, 9 * 128],
               "kaug3": [65, 9 * 128], "v0": [128, 256], "v8": [128, 256],
               "yacc": [128, NSLOT * D]}[nm]
        dt_ = F32 if nm in ("comb", "yacc") else BF16
        DBGAPS[nm] = nc.dram_tensor("dbg_" + nm, shp, dt_,
                                    kind="ExternalOutput").ap()

    counts = {}
    for nm, eng_attr in [("t", "tensor"), ("v", "vector"), ("s", "scalar"),
                         ("g", "gpsimd"), ("y", "sync")]:
        def body(eng, nm=nm):
            io = IO(sems, nm, eng)
            program(io)
            counts[nm] = dict(io.cnt)
        getattr(blk, eng_attr)(body)
    base = counts["t"]
    for nm in ["v", "s", "g", "y"]:
        assert counts[nm] == base, (nm, counts[nm], base)

    ctx.close()
    return nc


def host_constants(q):
    out = {}
    inv = 1.0 / (10000.0 ** (np.arange(0, 32, 2, dtype=np.float64) / 32.0))
    pos = np.arange(T, dtype=np.float64)
    ang = pos[None, :] * inv[:, None]
    cos = np.cos(ang).astype(np.float32)
    sin = np.sin(ang).astype(np.float32)
    ctab = np.zeros((128, T), np.float32)
    stab = np.zeros((128, T), np.float32)
    for hh in range(4):
        ctab[hh * 32:hh * 32 + 16] = cos
        ctab[hh * 32 + 16:hh * 32 + 32] = cos
        stab[hh * 32:hh * 32 + 16] = -sin
        stab[hh * 32 + 16:hh * 32 + 32] = sin
    out["csc"] = ctab.astype(ml_dtypes.bfloat16)
    out["css"] = stab.astype(ml_dtypes.bfloat16)
    own_tiles = [q + 4 * s for s in range(NSLOT)]
    own_cols = np.concatenate([np.arange(t0 * 128, t0 * 128 + 128)
                               for t0 in own_tiles])
    out["cso_c"] = np.ascontiguousarray(ctab[:, own_cols]).astype(ml_dtypes.bfloat16)
    out["cso_s"] = np.ascontiguousarray(stab[:, own_cols]).astype(ml_dtypes.bfloat16)
    out["negid"] = (-np.eye(128, dtype=np.float32)).astype(ml_dtypes.bfloat16)
    psw = np.zeros((128, 128), np.float32)
    for hh in range(4):
        for dd in range(16):
            psw[hh * 32 + dd, hh * 32 + 16 + dd] = 1.0
            psw[hh * 32 + 16 + dd, hh * 32 + dd] = 1.0
    out["pswap"] = psw.astype(ml_dtypes.bfloat16)

    kpos = np.zeros(9 * 128, np.int64)
    kreal = np.zeros(9 * 128, bool)
    kpos[0:48] = 64 * np.arange(48) + 63
    kreal[0:48] = True
    for j in range(8):
        kpos[128 * (1 + j):128 * (2 + j)] = 3072 + 128 * j + np.arange(128)
        kreal[128 * (1 + j):128 * (2 + j)] = True
    mstd = np.zeros((NSLOT, 9, 128, 128), np.float32)
    mtt = np.zeros((NSLOT, 9, 128, 128), np.float32)
    for s in range(NSLOT):
        t0 = own_tiles[s]
        qpos = t0 * 128 + np.arange(128)
        for c in range(9):
            kp = kpos[c * 128:(c + 1) * 128]
            kr = kreal[c * 128:(c + 1) * 128]
            visible = (kp[None, :] <= qpos[:, None]) & kr[None, :]
            m = np.where(visible, 0.0, NEG).astype(np.float32)
            mstd[s, c] = m
            mtt[s, c] = m.T
    out["mstd"] = mstd.astype(ml_dtypes.bfloat16)
    out["mtt"] = mtt.astype(ml_dtypes.bfloat16)

    degA = np.ones((128, 128), np.float32)
    degB = np.zeros((128, 128), np.float32)
    degA[64, :] = 0.0
    if q == 0:
        degA[:, 0:63] = 0.0
        degB[0:48, 0:63] = 1.0 / 1072.0
        degB[64, 0:63] = 1.0
    out["degA"] = degA.astype(ml_dtypes.bfloat16)
    out["degB"] = degB.astype(ml_dtypes.bfloat16)
    return out, own_cols


def make_in_maps(x, weights, lsx):
    in_maps = []
    own_cols_all = []
    for c in range(8):
        b, q = c // 4, c % 4
        consts, own_cols = host_constants(q)
        own_cols_all.append(own_cols)
        m = dict(weights)
        m["lsx"] = lsx
        m.update(consts)
        m["xb"] = x[b]
        m["x_own"] = np.ascontiguousarray(x[b][own_cols])
        in_maps.append(m)
    return in_maps, own_cols_all


_CACHE = {}


def kernel(L0_attn_x, L0_attn_Wq_sem, L0_attn_Wk_sem, L0_attn_Wq_geo,
           L0_attn_Wk_geo, L0_attn_Wv, L0_attn_Wo, L0_attn_logit_scale,
           L0_ffn_gate_w, L0_ffn_w1, L0_ffn_w2, L0_ffn_sw1, L0_ffn_sw2):
    x = np.ascontiguousarray(np.asarray(L0_attn_x, dtype=np.float32))
    ls = np.asarray(L0_attn_logit_scale, dtype=np.float32)
    lsx = np.zeros((128, 4), np.float32)
    for cq in range(4):
        for hh in range(4):
            lsx[hh * 32:(hh + 1) * 32, cq] = ls[4 * cq + hh]

    if "nc" not in _CACHE:
        _CACHE["nc"] = build_program()
    nc = _CACHE["nc"]

    f32c = lambda a: np.ascontiguousarray(np.asarray(a, np.float32))
    weights = {"wqs": f32c(L0_attn_Wq_sem), "wks": f32c(L0_attn_Wk_sem),
               "wqg": f32c(L0_attn_Wq_geo), "wkg": f32c(L0_attn_Wk_geo),
               "wv": f32c(L0_attn_Wv), "wo": f32c(L0_attn_Wo),
               "gw": f32c(L0_ffn_gate_w), "w1": f32c(L0_ffn_w1),
               "w2": f32c(L0_ffn_w2), "sw1": f32c(L0_ffn_sw1),
               "sw2": f32c(L0_ffn_sw2)}
    in_maps, own_cols_all = make_in_maps(x, weights, lsx)
    res = run_bass_kernel_spmd(nc, in_maps, core_ids=list(range(8)))
    y = np.zeros((B, T, D), np.float32)
    for c in range(8):
        b = c // 4
        y[b][own_cols_all[c]] = res.results[c]["y"]
    return y


# version 04
# speedup vs baseline: 1.6862x; 1.1404x over previous
import numpy as np
import ml_dtypes

import concourse.bass as bass
import concourse.mybir as mybir
from concourse.bass_utils import run_bass_kernel_spmd

F32 = mybir.dt.float32
BF16 = mybir.dt.bfloat16
AF = mybir.ActivationFunctionType
ALU = mybir.AluOpType
AX = mybir.AxisListType

D = 2048
T = 4096
B = 2
H = 16
HKV = 4
NPOOL = 48
NEG = -1.0e9
FFH = 1408
NE = 16
NSLOT = 8
VIS = [[0]] * 6 + [[0, 1, 2, 3, 4], list(range(9))]
SQS = float(1.0 / np.sqrt(32.0))


class _Dummy:
    def then_inc(self, *a, **k):
        return self


_DUM = _Dummy()


class Proxy:
    def __init__(self, eng):
        self.e = eng


class Res:
    __slots__ = ("name", "w", "readers")

    def __init__(self, name):
        self.name = name
        self.w = None
        self.readers = []


class IO:
    def __init__(self, sems, live_name, eng):
        self.cnt = {k: 0 for k in sems}
        self.sems = sems
        self.waited = {k: 0 for k in sems}
        for nm in ["t", "v", "s", "g", "y"]:
            setattr(self, nm, Proxy(eng if nm == live_name else None))
        self.pmap = {}
        self.pmap[id(self.t)] = "pe"
        self.pmap[id(self.v)] = "dve"
        self.pmap[id(self.s)] = "act"
        self.pmap[id(self.g)] = "pool"
        self.pmap[id(self.y)] = "spq"

    def n(self, cname):
        return self.cnt[cname]

    def wait(self, proxy, cname, val):
        if val <= 0:
            return
        if proxy.e is not None:
            if self.waited[cname] >= val:
                return
            proxy.e.wait_ge(self.sems[cname], val)
            self.waited[cname] = val

    def emit(self, proxy, fns, reads=(), writes=(), inc=1, cname=None):
        if cname is None:
            cname = self.pmap[id(proxy)]
        for r in reads:
            if r.w:
                self.wait(proxy, *r.w)
        for r in writes:
            if r.w:
                self.wait(proxy, *r.w)
            for rd in r.readers:
                self.wait(proxy, *rd)
        dma = cname in ("dx", "dxo", "dwa", "dwf", "dmk", "dtr", "dcs", "dou")
        if dma:
            assert inc == 16 * len(fns), (cname, inc, len(fns))
        last = _DUM
        if proxy.e is not None:
            for f in fns:
                last = f()
                if dma and last is not _DUM:
                    last.then_inc(self.sems[cname], 16)
        if (last is not _DUM) and not dma:
            last.then_inc(self.sems[cname], inc)
        self.cnt[cname] += inc
        mark = (cname, self.cnt[cname])
        for r in reads:
            r.readers.append(mark)
        for r in writes:
            r.w = mark
            r.readers = []
        return mark


def build_program(dbg=(), p3=True, p4=True, p3_slots=NSLOT):
    nc = bass.Bass("TRN2", target_bir_lowering=False, debug=False)

    def din(name, shape, dtype=F32):
        return nc.dram_tensor(name, list(shape), dtype, kind="ExternalInput").ap()

    I = {}
    I["xb"] = din("xb", [T, D])
    I["x_own"] = din("x_own", [NSLOT * 128, D])
    I["wqs"] = din("wqs", [D, 512])
    I["wks"] = din("wks", [D, 128])
    I["wqg"] = din("wqg", [D, 512])
    I["wkg"] = din("wkg", [D, 128])
    I["wv"] = din("wv", [D, 256])
    I["wo"] = din("wo", [1024, D])
    I["gw"] = din("gw", [NE, D])
    I["w1"] = din("w1", [NE, D, FFH])
    I["w2"] = din("w2", [NE, FFH, D])
    I["sw1"] = din("sw1", [1, D, FFH])
    I["sw2"] = din("sw2", [1, FFH, D])
    I["lsx"] = din("lsx", [128, 4])
    I["csc"] = din("csc", [128, T], BF16)
    I["css"] = din("css", [128, T], BF16)
    I["cso_c"] = din("cso_c", [128, NSLOT * 128], BF16)
    I["cso_s"] = din("cso_s", [128, NSLOT * 128], BF16)
    I["negid"] = din("negid", [128, 128], BF16)
    I["pswap"] = din("pswap", [128, 128], BF16)
    I["mstd"] = din("mstd", [NSLOT, 9, 128, 128], BF16)
    I["mtt"] = din("mtt", [NSLOT, 9, 128, 128], BF16)
    I["degA"] = din("degA", [128, 128], BF16)
    I["degB"] = din("degB", [128, 128], BF16)
    y_out = nc.dram_tensor("y", [NSLOT * 128, D], F32, kind="ExternalOutput").ap()

    from contextlib import ExitStack
    ctx = ExitStack()
    sball = lambda name, shape, dt: ctx.enter_context(
        nc.sbuf_tensor(name, list(shape), dt))

    y_acc = sball("y_acc", [128, NSLOT * D], F32)
    xn_own = sball("xn_own", [128, 16 * 1024], BF16)
    comb = sball("comb", [128, NSLOT * 16], F32)
    scol = sball("scol", [128, 4], F32)
    ones_col = sball("ones_col", [128, 1], BF16)
    zero_col = sball("zero_col", [128, 1], BF16)
    negid_sb = sball("negid_sb", [128, 128], BF16)
    pswap_sb = sball("pswap_sb", [128, 128], BF16)
    lsx_sb = sball("lsx_sb", [128, 4], F32)
    scr8 = sball("scr8", [128, 8], F32)
    lnscr2 = [sball(f"lnscr{i}", [128, 32], F32) for i in range(2)]
    bn6 = [t[:, 0:24] for t in lnscr2]
    bn2 = [t[:, 24:26] for t in lnscr2]
    bn2_mean = [t[:, 24:25] for t in lnscr2]
    bn2_var = [t[:, 25:26] for t in lnscr2]
    lnv = [t[:, 26:27] for t in lnscr2]
    rstd = [t[:, 27:28] for t in lnscr2]
    epsc = lnscr2[0][:, 29:30]

    ARENA = 112640
    arena = sball("arena", [128, ARENA], mybir.dt.uint8)
    arena_base = nc.lookup_mloc(arena).addr

    def at(name, shape, dtype, off):
        nbytes = int(np.prod(shape[1:])) * mybir.dt.size(dtype)
        assert off + nbytes <= ARENA, f"{name}: {off}+{nbytes} > {ARENA}"
        return nc.alloc_sbuf_tensor_at(name, list(shape), dtype,
                                       offset=arena_base + off)

    wks_sb = at("wks_sb", [128, 16 * 128], BF16, 0)
    wkg_sb = at("wkg_sb", [128, 16 * 128], BF16, 4096)
    wv_sb = at("wv_sb", [128, 16 * 256], BF16, 8192)
    xn_fm2 = [at(f"xn_fm{i}", [128, 16 * 512], BF16, 16384 + 16384 * i)
              for i in range(2)]
    cs_c = at("cs_c", [128, 512], BF16, 49152)
    cs_s = at("cs_s", [128, 512], BF16, 50176)
    kgraw = at("kgraw", [128, 512], BF16, 51200)
    kgsw = at("kgsw", [128, 512], BF16, 52224)
    kgro = at("kgro", [128, 512], BF16, 53248)
    pxn = at("pxn", [128, 16 * NPOOL], BF16, 54272)
    kgp = at("kgp", [128, NPOOL], F32, 55808)
    xstage = [at(f"xst{i}", [128, D], BF16, 56064 + i * 4096) for i in range(2)]
    xnstage = [at(f"xnst{i}", [128, D], BF16, 64256 + i * 4096) for i in range(2)]
    kaug = [at(f"kaug{g}", [65, 9 * 128], BF16, 72448 + g * 2304) for g in range(HKV)]
    v_sb = [at(f"v{c}", [128, 256], BF16, 81664 + c * 512) for c in range(9)]
    gw_sb = at("gw_sb", [16, D], BF16, 0)
    gwt = at("gwt", [128, 256], BF16, 4096)
    sgate = at("sgate", [128, 16], F32, 49152)
    gm1 = at("gm1", [128, 4], F32, 49216)
    grm = at("grm", [128, 4], F32, 49248)
    gs4 = at("gs4", [128, 4], F32, 49280)
    geq = at("geq", [128, 16], F32, 49312)
    gtmp = at("gtmp", [128, 16], F32, 49376)
    gmask = at("gmask", [128, 4], F32, 49440)
    emask = at("emask", [128, 16], F32, 49472)
    gmsk16 = at("gmsk16", [128, 16], F32, 49536)
    t1c = at("t1c", [128, 1], F32, 49600)
    rtc = at("rtc", [128, 1], F32, 49632)
    dinv = at("dinv", [128, 1], F32, 49664)
    wo_sb = at("wo_sb", [128, 8 * D], BF16, 0)
    wqs_sb = at("wqs_sb", [128, 16 * 512], BF16, 32768)
    wqg_sb = at("wqg_sb", [128, 16 * 512], BF16, 49152)
    cso_c_sb = at("cso_c_sb", [128, NSLOT * 128], BF16, 65536)
    cso_s_sb = at("cso_s_sb", [128, NSLOT * 128], BF16, 67584)
    mcst = at("mcst", [128, H * 9], F32, 69632)
    negm = at("negm", [128, H * 9], F32, 70208)
    dcst = at("dcst", [128, H * 9], F32, 70784)
    edst = at("edst", [128, H * 9], F32, 71360)
    mall = at("mall", [128, H], F32, 71936)
    dall = at("dall", [128, H], F32, 72000)
    mprime = at("mprime", [128, H], F32, 72064)
    mprbf = at("mprbf", [128, H], BF16, 72128)
    expT = [at(f"expT{i}", [128, H * 128], BF16, 86272 + i * 4096) for i in range(2)]
    expsc = at("expsc", [128, 1024], BF16, 94464)
    av_acc = at("av_acc", [64, H * 128], F32, 96512)
    mstd_sb = [at(f"mstd{i}", [128, 128], BF16, 104704 + i * 256) for i in range(2)]
    mtt_sb = [at(f"mtt{i}", [128, 128], BF16, 105216 + i * 256) for i in range(2)]
    qslot = at("qslot", [65, H * 128], BF16, 105728)
    degA_sb = at("degA_sb", [128, 128], BF16, 109824)
    degB_sb = at("degB_sb", [128, 128], BF16, 110080)
    qg1 = at("qg1", [128, 128], BF16, 110336)
    qg2 = at("qg2", [128, 128], BF16, 110592)
    w1_sb = at("w1_sb", [128, 16 * FFH], BF16, 0)
    w2_sb = at("w2_sb", [128, 11 * D], BF16, 45056)
    silu_sb = at("silu_sb", [128, 11 * 1024], BF16, 90112)

    ps_all = ctx.enter_context(nc.psum_tensor("ps_all", [128, 4096], F32))
    psA = ps_all[:, 0:512]
    psB = ps_all[:, 512:1024]
    psV = ps_all[:, 1024:1536]
    psQ = ps_all[:, 1536:2048]
    ps_big = ps_all[:, 0:2048]
    ps_av = ps_all[:, 2048:4096]
    ps1 = [ps_all[:, 0:1024], ps_all[:, 1024:2048]]
    ps2 = ps_all[:, 2048:4096]

    sem_names = ["dx", "dxo", "dwa", "dwf", "dmk", "dtr", "dcs", "dou",
                 "pe", "dve", "act", "pool", "spq"]
    sems = {nm: ctx.enter_context(nc.semaphore(nm)) for nm in sem_names}
    blk = ctx.enter_context(nc.Block())

    RNAMES = (["xstage0", "xstage1", "xnstage0", "xnstage1", "xnfm0", "xnfm1",
               "cs", "kgraw", "kgsw", "kgro", "pxn", "kgp", "bnst0", "bnst1",
               "xn_own", "qslot", "comb", "scol", "consts", "gwt", "gw_sb",
               "cso", "qgbuf", "wks", "wkg", "wv", "wqs", "wqg", "wo", "w1",
               "w2", "silu", "expT0", "expT1", "expsc", "masks", "deg",
               "stats", "route", "mpr", "avac",
               "b0", "b1", "b2", "b3", "b4", "b5", "b6", "b7"]
              + [f"kaug{g}" for g in range(HKV)]
              + [f"v{c}" for c in range(9)]
              + [f"yac{s}" for s in range(NSLOT)])
    BK = {"A": ["b0"], "B": ["b1"], "V": ["b2"], "Q": ["b3"],
          "big": ["b0", "b1", "b2", "b3"], "av": ["b4", "b5", "b6", "b7"],
          "p10": ["b0", "b1"], "p11": ["b2", "b3"], "av0": ["b4"]}

    def program(io: IO):
        t, v, s, g, y = io.t, io.v, io.s, io.g, io.y
        E = io.emit
        R = {nm: Res(nm) for nm in RNAMES}
        bks = lambda key: [R[b] for b in BK[key]]
        KA = [R[f"kaug{gg}"] for gg in range(HKV)]

        E(y, [lambda: y.e.dma_start(out=pswap_sb[:, :], in_=I["pswap"]),
              lambda: y.e.dma_start(out=negid_sb[:, :], in_=I["negid"]),
              lambda: y.e.dma_start(out=lsx_sb[:, :], in_=I["lsx"])],
          writes=[R["consts"]], inc=48, cname="dmk")
        E(g, [lambda: g.e.dma_start(
                out=wks_sb[:, :].rearrange("p (kc m) -> p kc m", m=128),
                in_=I["wks"].rearrange("(kc p) m -> p kc m", p=128))],
          writes=[R["wks"]], inc=16, cname="dwa")
        E(g, [lambda: g.e.dma_start(
                out=wkg_sb[:, :].rearrange("p (kc m) -> p kc m", m=128),
                in_=I["wkg"].rearrange("(kc p) m -> p kc m", p=128))],
          writes=[R["wkg"]], inc=16, cname="dwa")
        E(g, [lambda: g.e.dma_start(
                out=wv_sb[:, :].rearrange("p (kc m) -> p kc m", m=256),
                in_=I["wv"].rearrange("(kc p) m -> p kc m", p=128))],
          writes=[R["wv"]], inc=16, cname="dwa")
        E(v, [lambda: v.e.memset(ones_col[:, :], 1.0)], writes=[R["consts"]])
        E(v, [lambda: v.e.memset(zero_col[:, :], 0.0)], writes=[R["consts"]])
        E(v, [lambda: v.e.memset(epsc, 1.0e-5)],
          writes=[R["bnst0"], R["bnst1"]])
        for gg in range(HKV):
            E(v, [lambda gg=gg: v.e.memset(kaug[gg][64:65, :], 1.0),
                  lambda gg=gg: v.e.memset(kaug[gg][0:64, 48:128], 0.0)],
              writes=[KA[gg]])
        E(v, [lambda: v.e.memset(v_sb[0][:, :], 0.0)], writes=[R["v0"]])
        E(s, [lambda: s.e.activation(scol[:, :], lsx_sb[:, :], AF.Exp)],
          reads=[R["consts"]], writes=[R["scol"]])
        E(v, [lambda: v.e.tensor_scalar(scol[:, :], scol[:, :], SQS, None,
                                        ALU.mult)],
          reads=[R["scol"]], writes=[R["scol"]])

        state = {"tile": 0}

        def ln_tile(src_d, row0, dst, dres, col):
            ti = state["tile"]
            bufi = ti % 2
            xs, xns = xstage[bufi], xnstage[bufi]
            rxs, rxns = R[f"xstage{bufi}"], R[f"xnstage{bufi}"]
            rbn = R[f"bnst{bufi}"]
            dsem = "dx" if src_d is I["xb"] else "dxo"
            E(g, [lambda: g.e.dma_start(out=xs[:, :],
                                        in_=src_d[row0:row0 + 128, :])],
              writes=[rxs], inc=16, cname=dsem)
            for bc in range(4):
                E(v, [lambda bc=bc: v.e.bn_stats(
                        bn6[bufi][:, bc * 6:(bc + 1) * 6],
                        xs[:, bc * 512:(bc + 1) * 512])],
                  reads=[rxs], writes=[rbn])
            E(v, [lambda: v.e.bn_aggr(bn2[bufi], bn6[bufi])],
              reads=[rbn], writes=[rbn])
            E(s, [lambda: s.e.activation(lnv[bufi], bn2_var[bufi], AF.Ln,
                                         bias=epsc)],
              reads=[rbn], writes=[rbn])
            E(s, [lambda: s.e.activation(rstd[bufi], lnv[bufi], AF.Exp,
                                         scale=-0.5)],
              reads=[rbn], writes=[rbn])
            E(v, [lambda: v.e.tensor_scalar(xns[:, :], xs[:, :],
                                            bn2_mean[bufi], rstd[bufi],
                                            ALU.subtract, ALU.mult)],
              reads=[rxs, rbn], writes=[rxns])
            E(y, [lambda: y.e.dma_start(
                    out=dst.rearrange("p (kc t) -> p kc t", kc=16)[:, :, col:col + 128],
                    in_=xns[:, :].rearrange("p (kc t) -> p kc t", t=128),
                    transpose=True)],
              reads=[rxns], writes=[dres], inc=16, cname="dtr")
            state["tile"] += 1

        def emit_ln_hc(hc):
            base = hc * 512
            xb_fm = xn_fm2[hc % 2]
            rfm = R[f"xnfm{hc % 2}"]
            for j in range(4):
                ln_tile(I["xb"], base + j * 128, xb_fm[:, :], rfm, j * 128)

        def emit_proj_hc(hc):
            base = hc * 512
            xb_fm = xn_fm2[hc % 2]
            rfm = R[f"xnfm{hc % 2}"]
            E(y, [lambda: y.e.dma_start(out=cs_c[:, :],
                                        in_=I["csc"][:, base:base + 512]),
                  lambda: y.e.dma_start(out=cs_s[:, :],
                                        in_=I["css"][:, base:base + 512])],
              writes=[R["cs"]], inc=32, cname="dcs")
            E(t, [(lambda kc=kc: t.e.matmul(
                    psA[:, :], wkg_sb[:, kc * 128:(kc + 1) * 128],
                    xb_fm[:, kc * 512:(kc + 1) * 512],
                    start=(kc == 0), stop=(kc == 15))) for kc in range(16)],
              reads=[R["wkg"], rfm], writes=bks("A"))
            E(s, [lambda: s.e.activation(kgraw[:, :], psA[:, :], AF.Copy)],
              reads=bks("A"), writes=[R["kgraw"]])
            E(t, [lambda: t.e.matmul(psB[:, :], pswap_sb[:, :], kgraw[:, :],
                                     start=True, stop=True)],
              reads=[R["consts"], R["kgraw"]], writes=bks("B"))
            E(s, [lambda: s.e.activation(kgsw[:, :], psB[:, :], AF.Copy)],
              reads=bks("B"), writes=[R["kgsw"]])
            E(v, [lambda: v.e.tensor_tensor(kgraw[:, :], kgraw[:, :],
                                            cs_c[:, :], ALU.mult)],
              reads=[R["cs"]], writes=[R["kgraw"]])
            E(v, [lambda: v.e.tensor_tensor(kgro[:, :], kgsw[:, :],
                                            cs_s[:, :], ALU.mult)],
              reads=[R["kgsw"], R["cs"]], writes=[R["kgro"]])
            E(v, [lambda: v.e.tensor_tensor(kgro[:, :], kgro[:, :],
                                            kgraw[:, :], ALU.add)],
              reads=[R["kgraw"]], writes=[R["kgro"]])
            if hc < 6:
                E(v, [lambda: v.e.tensor_reduce(
                        kgp[:, hc * 8:(hc + 1) * 8],
                        kgro[:, :].rearrange("p (b w) -> p b w", w=64),
                        AX.X, ALU.add)],
                  reads=[R["kgro"]], writes=[R["kgp"]])
                for kc in range(16):
                    E(v, [lambda kc=kc: v.e.tensor_reduce(
                            scr8[:, :],
                            xb_fm[:, kc * 512:(kc + 1) * 512]
                            .rearrange("p (b w) -> p b w", w=64),
                            AX.X, ALU.add)],
                      reads=[rfm], writes=[R["qgbuf"]])
                    E(s, [lambda kc=kc: s.e.activation(
                            pxn[:, kc * NPOOL + hc * 8: kc * NPOOL + hc * 8 + 8],
                            scr8[:, :], AF.Copy)],
                      reads=[R["qgbuf"]], writes=[R["pxn"]])
            else:
                off = 128 + (base - 3072)
                for gg in range(HKV):
                    E(v, [lambda gg=gg: v.e.tensor_copy(
                            kaug[gg][32:64, off:off + 512],
                            kgro[32 * gg:32 * gg + 32, :])],
                      reads=[R["kgro"]], writes=[KA[gg]])
                E(t, [(lambda kc=kc: t.e.matmul(
                        psB[:, :], wks_sb[:, kc * 128:(kc + 1) * 128],
                        xb_fm[:, kc * 512:(kc + 1) * 512],
                        start=(kc == 0), stop=(kc == 15))) for kc in range(16)],
                  reads=[R["wks"], rfm], writes=bks("B"))
                for gg in range(HKV):
                    E(s, [lambda gg=gg: s.e.activation(
                            kaug[gg][0:32, off:off + 512],
                            psB[32 * gg:32 * gg + 32, :], AF.Copy)],
                      reads=bks("B"), writes=[KA[gg]])
                for j in range(4):
                    tj = (base - 3072) // 128 + j
                    E(t, [(lambda kc=kc, j=j: t.e.matmul(
                            psV[:, 0:256],
                            xb_fm[:, kc * 512 + j * 128: kc * 512 + j * 128 + 128],
                            wv_sb[:, kc * 256:(kc + 1) * 256],
                            start=(kc == 0), stop=(kc == 15)))
                          for kc in range(16)],
                      reads=[R["wv"], rfm], writes=bks("V"))
                    E(s, [lambda tj=tj: s.e.activation(
                            v_sb[1 + tj][:, :], psV[:, 0:256], AF.Copy)],
                      reads=bks("V"), writes=[R[f"v{1 + tj}"]])

        emit_ln_hc(0)
        emit_ln_hc(1)
        own_emitted = 0
        for hc in range(8):
            emit_proj_hc(hc)
            if hc + 2 < 8:
                emit_ln_hc(hc + 2)
            elif own_emitted < 8:
                for sl in range(own_emitted, own_emitted + 4):
                    E(y, [lambda sl=sl: y.e.dma_start(
                            out=y_acc[:, sl * D:(sl + 1) * D],
                            in_=I["x_own"][sl * 128:(sl + 1) * 128, :])],
                      writes=[R[f"yac{sl}"]], inc=16, cname="dxo")
                    ln_tile(I["x_own"], sl * 128, xn_own[:, :], R["xn_own"],
                            sl * 128)
                own_emitted += 4
        E(t, [(lambda kc=kc: t.e.matmul(
                psA[:, 0:NPOOL], wks_sb[:, kc * 128:(kc + 1) * 128],
                pxn[:, kc * NPOOL:(kc + 1) * NPOOL],
                start=(kc == 0), stop=(kc == 15))) for kc in range(16)],
          reads=[R["wks"], R["pxn"]], writes=bks("A"))
        for gg in range(HKV):
            E(s, [lambda gg=gg: s.e.activation(
                    kaug[gg][0:32, 0:NPOOL],
                    psA[32 * gg:32 * gg + 32, 0:NPOOL], AF.Copy,
                    scale=1.0 / 64.0)],
              reads=bks("A"), writes=[KA[gg]])
        E(t, [(lambda kc=kc: t.e.matmul(
                psV[0:48, 0:256], pxn[:, kc * NPOOL:(kc + 1) * NPOOL],
                wv_sb[:, kc * 256:(kc + 1) * 256],
                start=(kc == 0), stop=(kc == 15))) for kc in range(16)],
          reads=[R["wv"], R["pxn"]], writes=bks("V"))
        E(s, [lambda: s.e.activation(v_sb[0][0:48, :], psV[0:48, 0:256],
                                     AF.Copy, scale=1.0 / 64.0)],
          reads=bks("V"), writes=[R["v0"]])
        for gg in range(HKV):
            E(v, [lambda gg=gg: v.e.tensor_scalar(
                    kaug[gg][32:64, 0:NPOOL], kgp[32 * gg:32 * gg + 32, :],
                    1.0 / 64.0, None, ALU.mult)],
              reads=[R["kgp"]], writes=[KA[gg]])

        E(g, [lambda: g.e.dma_start(out=gw_sb[:, :], in_=I["gw"])],
          writes=[R["gw_sb"], R["wks"]], inc=16, cname="dwa")
        E(t, [(lambda kc=kc: t.e.matmul(
                (psA if kc < 8 else psB)[:, (kc % 8) * 16:(kc % 8) * 16 + 16],
                gw_sb[:, kc * 128:(kc + 1) * 128], negid_sb[0:16, 0:16],
                start=True, stop=True)) for kc in range(16)],
          reads=[R["gw_sb"], R["consts"]], writes=bks("A") + bks("B"))
        E(s, [lambda: s.e.activation(gwt[:, 0:128], psA[:, 0:128], AF.Copy,
                                     scale=-1.0),
              lambda: s.e.activation(gwt[:, 128:256], psB[:, 0:128], AF.Copy,
                                     scale=-1.0)],
          reads=bks("A") + bks("B"), writes=[R["gwt"]])

        E(v, [lambda: v.e.memset(sgate[:, :], 0.0)],
          writes=[R["route"], R["cs"]])
        for sl in range(NSLOT):
            E(t, [(lambda kc=kc, sl=sl: t.e.matmul(
                    psQ[:, 0:16],
                    xn_own[:, kc * 1024 + sl * 128: kc * 1024 + sl * 128 + 128],
                    gwt[:, kc * 16:(kc + 1) * 16],
                    start=(kc == 0), stop=(kc == 15))) for kc in range(16)],
              reads=[R["xn_own"], R["gwt"]], writes=bks("Q"))
            E(s, [lambda: s.e.activation(sgate[:, :], psQ[:, 0:16], AF.Sigmoid)],
              reads=bks("Q"), writes=[R["route"]])
            rt = R["route"]

            def V1(f):
                E(v, [f], reads=[rt], writes=[rt])
            for gg in range(4):
                V1(lambda gg=gg: v.e.tensor_reduce(
                    gm1[:, gg:gg + 1], sgate[:, 4 * gg:4 * gg + 4], AX.X, ALU.max))
            for gg in range(4):
                V1(lambda gg=gg: v.e.tensor_scalar(
                    geq[:, 4 * gg:4 * gg + 4], sgate[:, 4 * gg:4 * gg + 4],
                    gm1[:, gg:gg + 1], None, ALU.is_equal))
                V1(lambda gg=gg: v.e.scalar_tensor_tensor(
                    gtmp[:, 4 * gg:4 * gg + 4], geq[:, 4 * gg:4 * gg + 4], 1.0e30,
                    sgate[:, 4 * gg:4 * gg + 4], ALU.mult, ALU.subtract))
                V1(lambda gg=gg: v.e.tensor_reduce(
                    grm[:, gg:gg + 1], gtmp[:, 4 * gg:4 * gg + 4], AX.X, ALU.min))
            V1(lambda: v.e.tensor_tensor(gs4[:, :], gm1[:, :], grm[:, :],
                                         ALU.subtract))
            V1(lambda: v.e.tensor_reduce(t1c[:, :], gs4[:, :], AX.X, ALU.max))
            V1(lambda: v.e.tensor_scalar(geq[:, 0:4], gs4[:, :], t1c[:, 0:1],
                                         None, ALU.is_equal))
            V1(lambda: v.e.scalar_tensor_tensor(gtmp[:, 0:4], geq[:, 0:4],
                                                1.0e30, gs4[:, :], ALU.mult,
                                                ALU.subtract))
            V1(lambda: v.e.tensor_reduce(rtc[:, :], gtmp[:, 0:4], AX.X, ALU.min))
            V1(lambda: v.e.tensor_scalar(gmask[:, :], gs4[:, :], -1.0,
                                         rtc[:, 0:1], ALU.mult, ALU.is_le))
            V1(lambda: v.e.tensor_copy(
                emask[:, :].rearrange("p (a b) -> p a b", b=4),
                gmask[:, :].rearrange("p (a b) -> p a b", b=1)
                .broadcast_to([128, 4, 4])))
            V1(lambda: v.e.tensor_tensor(gmsk16[:, :], sgate[:, :], emask[:, :],
                                         ALU.mult))
            V1(lambda: v.e.tensor_reduce(t1c[:, :], gmsk16[:, :], AX.X, ALU.max))
            V1(lambda: v.e.tensor_scalar(geq[:, :], gmsk16[:, :], t1c[:, 0:1],
                                         None, ALU.is_equal))
            V1(lambda: v.e.scalar_tensor_tensor(gtmp[:, :], geq[:, :], 1.0e30,
                                                gmsk16[:, :], ALU.mult,
                                                ALU.subtract))
            V1(lambda: v.e.tensor_reduce(rtc[:, :], gtmp[:, :], AX.X, ALU.min))
            V1(lambda: v.e.tensor_scalar(gtmp[:, :], gtmp[:, :], rtc[:, 0:1],
                                         None, ALU.is_equal))
            V1(lambda: v.e.tensor_tensor(geq[:, :], geq[:, :], gtmp[:, :],
                                         ALU.add))
            V1(lambda: v.e.tensor_tensor(gtmp[:, :], gmsk16[:, :], geq[:, :],
                                         ALU.mult))
            V1(lambda: v.e.tensor_tensor(t1c[:, :], t1c[:, :], rtc[:, 0:1],
                                         ALU.subtract))
            V1(lambda: v.e.tensor_scalar(t1c[:, :], t1c[:, :], 1.0e-20, None,
                                         ALU.add))
            V1(lambda: v.e.reciprocal(dinv[:, :], t1c[:, :]))
            E(v, [lambda sl=sl: v.e.tensor_scalar(
                    comb[:, sl * 16:(sl + 1) * 16], gtmp[:, :], dinv[:, 0:1],
                    None, ALU.mult)],
              reads=[rt], writes=[R["comb"]])

        if p3:
            E(g, [lambda: g.e.dma_start(
                    out=wo_sb[:, :].rearrange("p (kc m) -> p kc m", m=D),
                    in_=I["wo"].rearrange("(kc p) m -> p kc m", p=128))],
              writes=[R["wo"], R["gw_sb"], R["gwt"], R["wkg"], R["wv"],
                      R["xnfm0"]], inc=16, cname="dwa")
            E(g, [lambda: g.e.dma_start(
                    out=wqs_sb[:, :].rearrange("p (kc m) -> p kc m", m=512),
                    in_=I["wqs"].rearrange("(kc p) m -> p kc m", p=128))],
              writes=[R["wqs"], R["xnfm1"]], inc=16, cname="dwa")
            E(g, [lambda: g.e.dma_start(
                    out=wqg_sb[:, :].rearrange("p (kc m) -> p kc m", m=512),
                    in_=I["wqg"].rearrange("(kc p) m -> p kc m", p=128))],
              writes=[R["wqg"], R["route"], R["cs"], R["kgraw"], R["kgsw"],
                      R["kgro"], R["pxn"], R["kgp"], R["xstage0"],
                      R["xstage1"], R["xnstage0"]], inc=16, cname="dwa")
            E(y, [lambda: y.e.dma_start(out=cso_c_sb[:, :], in_=I["cso_c"]),
                  lambda: y.e.dma_start(out=cso_s_sb[:, :], in_=I["cso_s"])],
              writes=[R["cso"], R["xnstage0"], R["xnstage1"]], inc=32,
              cname="dcs")
            E(y, [lambda: y.e.dma_start(out=degA_sb[:, :], in_=I["degA"]),
                  lambda: y.e.dma_start(out=degB_sb[:, :], in_=I["degB"])],
              writes=[R["deg"]], inc=32, cname="dmk")
            E(t, [(lambda c=c: t.e.matmul(
                    ps_av[0:1, 0:256], ones_col[:, :], v_sb[c][:, :],
                    start=(c == 1), stop=(c == 8))) for c in range(1, 9)],
              reads=[R["consts"]] + [R[f"v{c}"] for c in range(1, 9)],
              writes=bks("av0"))
            E(s, [lambda: s.e.activation(v_sb[0][64:65, :], ps_av[0:1, 0:256],
                                         AF.Copy, scale=1.0 / 1072.0)],
              reads=bks("av0"), writes=[R["v0"]])

            for sl in range(p3_slots):
                vis = VIS[sl]
                nv = len(vis)
                for cq in range(4):
                    E(t, [(lambda kc=kc, cq=cq, sl=sl: t.e.matmul(
                            psQ[:, 0:128],
                            wqs_sb[:, kc * 512 + cq * 128: kc * 512 + (cq + 1) * 128],
                            xn_own[:, kc * 1024 + sl * 128: kc * 1024 + sl * 128 + 128],
                            start=(kc == 0), stop=(kc == 15))) for kc in range(16)],
                      reads=[R["wqs"], R["xn_own"]], writes=bks("Q"))
                    for hh in range(4):
                        h = 4 * cq + hh
                        E(s, [lambda h=h, hh=hh, cq=cq: s.e.activation(
                                qslot[0:32, h * 128:(h + 1) * 128],
                                psQ[32 * hh:32 * hh + 32, 0:128], AF.Copy,
                                scale=scol[32 * hh:32 * hh + 32, cq:cq + 1])],
                          reads=bks("Q") + [R["scol"]], writes=[R["qslot"]])
                    E(t, [(lambda kc=kc, cq=cq, sl=sl: t.e.matmul(
                            psQ[:, 0:128],
                            wqg_sb[:, kc * 512 + cq * 128: kc * 512 + (cq + 1) * 128],
                            xn_own[:, kc * 1024 + sl * 128: kc * 1024 + sl * 128 + 128],
                            start=(kc == 0), stop=(kc == 15))) for kc in range(16)],
                      reads=[R["wqg"], R["xn_own"]], writes=bks("Q"))
                    E(s, [lambda cq=cq: s.e.activation(
                            qg1[:, :], psQ[:, 0:128], AF.Copy,
                            scale=scol[:, cq:cq + 1])],
                      reads=bks("Q") + [R["scol"]], writes=[R["qgbuf"]])
                    E(t, [lambda: t.e.matmul(psB[:, 0:128], pswap_sb[:, :],
                                             qg1[:, :], start=True, stop=True)],
                      reads=[R["consts"], R["qgbuf"]], writes=bks("B"))
                    E(s, [lambda: s.e.activation(qg2[:, :], psB[:, 0:128],
                                                 AF.Copy)],
                      reads=bks("B"), writes=[R["qgbuf"]])
                    E(v, [lambda sl=sl: v.e.tensor_tensor(
                            qg1[:, :], qg1[:, :],
                            cso_c_sb[:, sl * 128:(sl + 1) * 128], ALU.mult)],
                      reads=[R["cso"]], writes=[R["qgbuf"]])
                    E(v, [lambda sl=sl: v.e.tensor_tensor(
                            qg2[:, :], qg2[:, :],
                            cso_s_sb[:, sl * 128:(sl + 1) * 128], ALU.mult)],
                      reads=[R["cso"]], writes=[R["qgbuf"]])
                    E(v, [lambda: v.e.tensor_tensor(qg1[:, :], qg1[:, :],
                                                    qg2[:, :], ALU.add)],
                      writes=[R["qgbuf"]])
                    for hh in range(4):
                        h = 4 * cq + hh
                        E(v, [lambda h=h, hh=hh: v.e.tensor_copy(
                                qslot[32:64, h * 128:(h + 1) * 128],
                                qg1[32 * hh:32 * hh + 32, :])],
                          reads=[R["qgbuf"]], writes=[R["qslot"]])

                E(v, [lambda: v.e.memset(mcst[:, :], -3.0e38),
                      lambda: v.e.memset(dcst[:, :], 0.0)],
                  writes=[R["stats"]])
                for ci, c in enumerate(vis):
                    mb = mstd_sb[ci % 2]
                    mrk = R["masks"]
                    E(y, [lambda sl=sl, c=c, mb=mb: y.e.dma_start(
                            out=mb[:, :], in_=I["mstd"][sl, c])],
                      writes=[mrk], inc=16, cname="dmk")
                    E(t, [(lambda h=h, c=c: t.e.matmul(
                            ps_big[:, h * 128:(h + 1) * 128],
                            qslot[0:64, h * 128:(h + 1) * 128],
                            kaug[h // 4][0:64, c * 128:(c + 1) * 128],
                            start=True, stop=True)) for h in range(H)],
                      reads=[R["qslot"]] + KA, writes=bks("big"))
                    E(v, [lambda mb=mb: v.e.tensor_tensor(
                            ps_big.rearrange("p (h k) -> p h k", k=128),
                            ps_big.rearrange("p (h k) -> p h k", k=128),
                            mb[:, :].rearrange("p (a k) -> p a k", a=1)
                            .broadcast_to([128, H, 128]), ALU.add)],
                      reads=[mrk], writes=bks("big"))
                    E(v, [lambda c=c: v.e.tensor_reduce(
                            mcst[:, :].rearrange("p (h c) -> p h c", c=9)[:, :, c:c + 1],
                            ps_big.rearrange("p (h k) -> p h k", k=128),
                            AX.X, ALU.max)],
                      reads=bks("big"), writes=[R["stats"]])
                    E(v, [lambda: v.e.tensor_scalar(
                            negm[:, :], mcst[:, :], -1.0, None, ALU.mult)],
                      reads=[R["stats"]], writes=[R["stats"]])
                    for h in range(H):
                        E(s, [lambda h=h, c=c: s.e.activation(
                                expT[0][:, h * 128:(h + 1) * 128],
                                ps_big[:, h * 128:(h + 1) * 128], AF.Exp,
                                bias=negm[:, h * 9 + c:h * 9 + c + 1],
                                accum_out=dcst[:, h * 9 + c:h * 9 + c + 1])],
                          reads=bks("big") + [R["stats"]],
                          writes=[R["expT0"], R["stats"]])

                E(v, [lambda: v.e.tensor_reduce(
                        mall[:, :], mcst[:, :].rearrange("p (h c) -> p h c", c=9),
                        AX.X, ALU.max)], reads=[R["stats"]], writes=[R["stats"]])
                E(v, [lambda: v.e.tensor_tensor(
                        edst[:, :].rearrange("p (h c) -> p h c", c=9),
                        mcst[:, :].rearrange("p (h c) -> p h c", c=9),
                        mall[:, :].rearrange("p (h c) -> p h c", c=1)
                        .broadcast_to([128, H, 9]), ALU.subtract)],
                  reads=[R["stats"]], writes=[R["stats"]])
                E(s, [lambda: s.e.activation(edst[:, :], edst[:, :], AF.Exp)],
                  reads=[R["stats"]], writes=[R["stats"]])
                E(v, [lambda: v.e.tensor_tensor(edst[:, :], edst[:, :],
                                                dcst[:, :], ALU.mult)],
                  reads=[R["stats"]], writes=[R["stats"]])
                E(v, [lambda: v.e.tensor_reduce(
                        dall[:, :], edst[:, :].rearrange("p (h c) -> p h c", c=9),
                        AX.X, ALU.add)], reads=[R["stats"]], writes=[R["stats"]])
                E(s, [lambda: s.e.activation(dall[:, :], dall[:, :], AF.Ln)],
                  reads=[R["stats"]], writes=[R["stats"]])
                E(v, [lambda: v.e.tensor_tensor(mprime[:, :], mall[:, :],
                                                dall[:, :], ALU.add)],
                  reads=[R["stats"]], writes=[R["mpr"]])
                E(v, [lambda: v.e.tensor_copy(mprbf[:, :], mprime[:, :])],
                  reads=[R["mpr"]], writes=[R["mpr"]])
                E(t, [(lambda h=h: t.e.matmul(
                        ps_av[0:1, h * 128:(h + 1) * 128], mprbf[:, h:h + 1],
                        negid_sb[:, :], start=True, stop=True))
                      for h in range(H)],
                  reads=[R["mpr"], R["consts"]], writes=bks("av"))
                E(v, [(lambda h=h: v.e.tensor_copy(
                        qslot[64:65, h * 128:(h + 1) * 128],
                        ps_av[0:1, h * 128:(h + 1) * 128])) for h in range(H)],
                  reads=bks("av"), writes=[R["qslot"]])
                if nv > 1:
                    E(v, [lambda: v.e.memset(av_acc[:, :], 0.0)],
                      writes=[R["avac"]])

                for ci, c in enumerate(vis):
                    eb, ebres = expT[ci % 2], R[f"expT{ci % 2}"]
                    mb = mtt_sb[ci % 2]
                    mrk = R["masks"]
                    E(y, [lambda sl=sl, c=c, mb=mb: y.e.dma_start(
                            out=mb[:, :], in_=I["mtt"][sl, c])],
                      writes=[mrk], inc=16, cname="dmk")
                    E(t, [(lambda h=h, c=c: t.e.matmul(
                            ps_big[:, h * 128:(h + 1) * 128],
                            kaug[h // 4][0:65, c * 128:(c + 1) * 128],
                            qslot[0:65, h * 128:(h + 1) * 128],
                            start=True, stop=True)) for h in range(H)],
                      reads=[R["qslot"]] + KA, writes=bks("big"))
                    E(v, [lambda mb=mb: v.e.tensor_tensor(
                            ps_big.rearrange("p (h k) -> p h k", k=128),
                            ps_big.rearrange("p (h k) -> p h k", k=128),
                            mb[:, :].rearrange("p (a k) -> p a k", a=1)
                            .broadcast_to([128, H, 128]), ALU.add)],
                      reads=[mrk], writes=bks("big"))
                    E(s, [lambda eb=eb: s.e.activation(eb[:, :], ps_big[:, :],
                                                       AF.Exp)],
                      reads=bks("big"), writes=[ebres])
                    if sl == 0 and ci == 0:
                        E(v, [lambda eb=eb: v.e.tensor_tensor(
                                eb[:, :].rearrange("p (h k) -> p h k", k=128),
                                eb[:, :].rearrange("p (h k) -> p h k", k=128),
                                degA_sb[:, :].rearrange("p (a k) -> p a k", a=1)
                                .broadcast_to([128, H, 128]), ALU.mult)],
                          reads=[R["deg"]], writes=[ebres])
                        E(v, [lambda eb=eb: v.e.tensor_tensor(
                                eb[:, :].rearrange("p (h k) -> p h k", k=128),
                                eb[:, :].rearrange("p (h k) -> p h k", k=128),
                                degB_sb[:, :].rearrange("p (a k) -> p a k", a=1)
                                .broadcast_to([128, H, 128]), ALU.add)],
                          reads=[R["deg"]], writes=[ebres])
                    E(t, [(lambda h=h, c=c, eb=eb: t.e.matmul(
                            ps_av[0:64, h * 128:(h + 1) * 128],
                            v_sb[c][:, (h // 4) * 64:(h // 4) * 64 + 64],
                            eb[:, h * 128:(h + 1) * 128],
                            start=True, stop=True)) for h in range(H)],
                      reads=[ebres, R[f"v{c}"]], writes=bks("av"))
                    if nv > 1:
                        E(v, [lambda: v.e.scalar_tensor_tensor(
                                av_acc[:, :], ps_av[0:64, :], 1.0,
                                av_acc[:, :], ALU.mult, ALU.add)],
                          reads=bks("av"), writes=[R["avac"]])

                if nv > 1:
                    E(s, [(lambda h=h: s.e.activation(
                            expsc[(h % 2) * 64:(h % 2) * 64 + 64,
                                  (h // 2) * 128:(h // 2) * 128 + 128],
                            av_acc[0:64, h * 128:(h + 1) * 128], AF.Copy))
                          for h in range(H)],
                      reads=[R["avac"]], writes=[R["expsc"]])
                else:
                    E(s, [(lambda h=h: s.e.activation(
                            expsc[(h % 2) * 64:(h % 2) * 64 + 64,
                                  (h // 2) * 128:(h // 2) * 128 + 128],
                            ps_av[0:64, h * 128:(h + 1) * 128], AF.Copy))
                          for h in range(H)],
                      reads=bks("av"), writes=[R["expsc"]])
                for oc in range(4):
                    E(t, [(lambda hd=hd, oc=oc: t.e.matmul(
                            ps_big[:, oc * 512:(oc + 1) * 512],
                            expsc[:, hd * 128:(hd + 1) * 128],
                            wo_sb[:, hd * D + oc * 512: hd * D + (oc + 1) * 512],
                            start=(hd == 0), stop=(hd == 7))) for hd in range(8)],
                      reads=[R["expsc"], R["wo"]], writes=[R[f"b{oc}"]])
                E(v, [lambda sl=sl: v.e.tensor_tensor(
                        y_acc[:, sl * D:(sl + 1) * D],
                        y_acc[:, sl * D:(sl + 1) * D], ps_big[:, 0:2048],
                        ALU.add)],
                  reads=bks("big"), writes=[R[f"yac{sl}"]])

        if p4:
            experts = [("sw1", "sw2", 0, None)] + \
                      [("w1", "w2", e, e) for e in range(NE)]
            NXP = len(experts)
            ali1 = [R["wo"], R["wqs"]]
            ali2 = [R["wqs"], R["wqg"], R["cso"], R["route"], R["cs"],
                    R["kgraw"], R["kgsw"], R["kgro"], R["pxn"], R["kgp"],
                    R["xstage0"], R["xstage1"], R["xnstage0"], R["xnstage1"],
                    R["expT0"], R["stats"], R["mpr"]] + KA + \
                   [R[f"v{c}"] for c in range(9)]
            ali3 = [R["expT0"], R["expT1"], R["expsc"], R["avac"], R["masks"],
                    R["stats"], R["mpr"], R["qslot"], R["deg"], R["qgbuf"]]

            def emit_load_w1(ei):
                n1, _, eidx, _ = experts[ei]
                E(g, [lambda n1=n1, eidx=eidx: g.e.dma_start(
                        out=w1_sb[:, :].rearrange("p (kc h) -> p kc h", h=FFH),
                        in_=I[n1][eidx].rearrange("(kc p) h -> p kc h", p=128))],
                  writes=[R["w1"]] + (ali1 if ei == 0 else []), inc=16,
                  cname="dwf")

            def emit_load_w2(ei):
                _, n2, eidx, _ = experts[ei]
                E(g, [lambda n2=n2, eidx=eidx: g.e.dma_start(
                        out=w2_sb[:, :].rearrange("p (hc n) -> p hc n", n=D),
                        in_=I[n2][eidx].rearrange("(hc p) n -> p hc n", p=128))],
                  writes=[R["w2"]] + (ali2 if ei == 0 else []), inc=16,
                  cname="dwf")

            def emit_mm1(ei, hcx):
                pk = "p10" if hcx % 2 == 0 else "p11"
                E(t, [(lambda kc=kc, th=th, hcx=hcx: t.e.matmul(
                        ps1[hcx % 2][:, th * 512:(th + 1) * 512],
                        w1_sb[:, kc * FFH + hcx * 128: kc * FFH + (hcx + 1) * 128],
                        xn_own[:, kc * 1024 + th * 512: kc * 1024 + (th + 1) * 512],
                        start=(kc == 0), stop=(kc == 15)))
                      for kc in range(16) for th in range(2)],
                  reads=[R["w1"], R["xn_own"]], writes=bks(pk))
                E(s, [lambda hcx=hcx: s.e.activation(
                        silu_sb[:, hcx * 1024:(hcx + 1) * 1024],
                        ps1[hcx % 2][:, :], AF.Silu)],
                  reads=bks(pk),
                  writes=[R["silu"]] + (ali3 if ei == 0 and hcx == 0 else []))

            def emit_mm2(ei, tc):
                ce = experts[ei][3]
                for half in range(2):
                    bkh = ["b4", "b5"] if half == 0 else ["b6", "b7"]
                    ocs = (0, 1) if half == 0 else (2, 3)
                    E(t, [(lambda hcx=hcx, oc=oc, tc=tc: t.e.matmul(
                            ps2[:, oc * 512:(oc + 1) * 512],
                            silu_sb[:, hcx * 1024 + tc * 128: hcx * 1024 + tc * 128 + 128],
                            w2_sb[:, hcx * D + oc * 512: hcx * D + (oc + 1) * 512],
                            start=(hcx == 0), stop=(hcx == 10)))
                          for hcx in range(11) for oc in ocs],
                      reads=[R["silu"], R["w2"]], writes=[R[b] for b in bkh])
                    if ce is None:
                        E(v, [lambda tc=tc, half=half: v.e.scalar_tensor_tensor(
                                y_acc[:, tc * D + half * 1024: tc * D + half * 1024 + 1024],
                                ps2[:, half * 1024:(half + 1) * 1024], 1.0,
                                y_acc[:, tc * D + half * 1024: tc * D + half * 1024 + 1024],
                                ALU.mult, ALU.add)],
                          reads=[R[b] for b in bkh], writes=[R[f"yac{tc}"]])
                    else:
                        E(v, [lambda tc=tc, ce=ce, half=half: v.e.scalar_tensor_tensor(
                                y_acc[:, tc * D + half * 1024: tc * D + half * 1024 + 1024],
                                ps2[:, half * 1024:(half + 1) * 1024],
                                comb[:, tc * 16 + ce:tc * 16 + ce + 1],
                                y_acc[:, tc * D + half * 1024: tc * D + half * 1024 + 1024],
                                ALU.mult, ALU.add)],
                          reads=[R[b] for b in bkh] + [R["comb"]],
                          writes=[R[f"yac{tc}"]])

            emit_load_w1(0)
            emit_load_w2(0)
            for hcx in range(11):
                emit_mm1(0, hcx)
            for ei in range(NXP):
                if ei + 1 < NXP:
                    emit_load_w1(ei + 1)
                for tc in range(8):
                    emit_mm2(ei, tc)
                if ei + 1 < NXP:
                    emit_load_w2(ei + 1)
                    for hcx in range(11):
                        emit_mm1(ei + 1, hcx)

        for sl in range(NSLOT):
            E(y, [lambda sl=sl: y.e.dma_start(
                    out=y_out[sl * 128:(sl + 1) * 128, :],
                    in_=y_acc[:, sl * D:(sl + 1) * D])],
              reads=[R[f"yac{sl}"]], inc=16, cname="dou")
        for nm in dbg:
            tgt = {"xn_own": xn_own, "qslot": qslot, "comb": comb,
                   "kaug0": kaug[0], "kaug3": kaug[3], "v0": v_sb[0],
                   "v8": v_sb[8], "yacc": y_acc}[nm]
            E(y, [lambda tgt=tgt, nm=nm: y.e.dma_start(
                    out=DBGAPS[nm], in_=tgt[:, :])], inc=16, cname="dou")
        io.wait(y, "dou", io.n("dou"))
        return io

    DBGAPS = {}
    for nm in dbg:
        shp = {"xn_own": [128, 16 * 1024], "qslot": [65, H * 128],
               "comb": [128, NSLOT * 16], "kaug0": [65, 9 * 128],
               "kaug3": [65, 9 * 128], "v0": [128, 256], "v8": [128, 256],
               "yacc": [128, NSLOT * D]}[nm]
        dt_ = F32 if nm in ("comb", "yacc") else BF16
        DBGAPS[nm] = nc.dram_tensor("dbg_" + nm, shp, dt_,
                                    kind="ExternalOutput").ap()

    counts = {}
    for nm, eng_attr in [("t", "tensor"), ("v", "vector"), ("s", "scalar"),
                         ("g", "gpsimd"), ("y", "sync")]:
        def body(eng, nm=nm):
            io = IO(sems, nm, eng)
            program(io)
            counts[nm] = dict(io.cnt)
        getattr(blk, eng_attr)(body)
    base = counts["t"]
    for nm in ["v", "s", "g", "y"]:
        assert counts[nm] == base, (nm, counts[nm], base)

    ctx.close()
    return nc


def host_constants(q):
    out = {}
    inv = 1.0 / (10000.0 ** (np.arange(0, 32, 2, dtype=np.float64) / 32.0))
    pos = np.arange(T, dtype=np.float64)
    ang = pos[None, :] * inv[:, None]
    cos = np.cos(ang).astype(np.float32)
    sin = np.sin(ang).astype(np.float32)
    ctab = np.zeros((128, T), np.float32)
    stab = np.zeros((128, T), np.float32)
    for hh in range(4):
        ctab[hh * 32:hh * 32 + 16] = cos
        ctab[hh * 32 + 16:hh * 32 + 32] = cos
        stab[hh * 32:hh * 32 + 16] = -sin
        stab[hh * 32 + 16:hh * 32 + 32] = sin
    out["csc"] = ctab.astype(ml_dtypes.bfloat16)
    out["css"] = stab.astype(ml_dtypes.bfloat16)
    own_tiles = [q + 4 * s for s in range(NSLOT)]
    own_cols = np.concatenate([np.arange(t0 * 128, t0 * 128 + 128)
                               for t0 in own_tiles])
    out["cso_c"] = np.ascontiguousarray(ctab[:, own_cols]).astype(ml_dtypes.bfloat16)
    out["cso_s"] = np.ascontiguousarray(stab[:, own_cols]).astype(ml_dtypes.bfloat16)
    out["negid"] = (-np.eye(128, dtype=np.float32)).astype(ml_dtypes.bfloat16)
    psw = np.zeros((128, 128), np.float32)
    for hh in range(4):
        for dd in range(16):
            psw[hh * 32 + dd, hh * 32 + 16 + dd] = 1.0
            psw[hh * 32 + 16 + dd, hh * 32 + dd] = 1.0
    out["pswap"] = psw.astype(ml_dtypes.bfloat16)

    kpos = np.zeros(9 * 128, np.int64)
    kreal = np.zeros(9 * 128, bool)
    kpos[0:48] = 64 * np.arange(48) + 63
    kreal[0:48] = True
    for j in range(8):
        kpos[128 * (1 + j):128 * (2 + j)] = 3072 + 128 * j + np.arange(128)
        kreal[128 * (1 + j):128 * (2 + j)] = True
    mstd = np.zeros((NSLOT, 9, 128, 128), np.float32)
    mtt = np.zeros((NSLOT, 9, 128, 128), np.float32)
    for s in range(NSLOT):
        t0 = own_tiles[s]
        qpos = t0 * 128 + np.arange(128)
        for c in range(9):
            kp = kpos[c * 128:(c + 1) * 128]
            kr = kreal[c * 128:(c + 1) * 128]
            visible = (kp[None, :] <= qpos[:, None]) & kr[None, :]
            m = np.where(visible, 0.0, NEG).astype(np.float32)
            mstd[s, c] = m
            mtt[s, c] = m.T
    out["mstd"] = mstd.astype(ml_dtypes.bfloat16)
    out["mtt"] = mtt.astype(ml_dtypes.bfloat16)

    degA = np.ones((128, 128), np.float32)
    degB = np.zeros((128, 128), np.float32)
    degA[64, :] = 0.0
    if q == 0:
        degA[:, 0:63] = 0.0
        degB[0:48, 0:63] = 1.0 / 1072.0
        degB[64, 0:63] = 1.0
    out["degA"] = degA.astype(ml_dtypes.bfloat16)
    out["degB"] = degB.astype(ml_dtypes.bfloat16)
    return out, own_cols


def make_in_maps(x, weights, lsx):
    in_maps = []
    own_cols_all = []
    for c in range(8):
        b, q = c // 4, c % 4
        consts, own_cols = host_constants(q)
        own_cols_all.append(own_cols)
        m = dict(weights)
        m["lsx"] = lsx
        m.update(consts)
        m["xb"] = x[b]
        m["x_own"] = np.ascontiguousarray(x[b][own_cols])
        in_maps.append(m)
    return in_maps, own_cols_all


_CACHE = {}


def kernel(L0_attn_x, L0_attn_Wq_sem, L0_attn_Wk_sem, L0_attn_Wq_geo,
           L0_attn_Wk_geo, L0_attn_Wv, L0_attn_Wo, L0_attn_logit_scale,
           L0_ffn_gate_w, L0_ffn_w1, L0_ffn_w2, L0_ffn_sw1, L0_ffn_sw2):
    x = np.ascontiguousarray(np.asarray(L0_attn_x, dtype=np.float32))
    ls = np.asarray(L0_attn_logit_scale, dtype=np.float32)
    lsx = np.zeros((128, 4), np.float32)
    for cq in range(4):
        for hh in range(4):
            lsx[hh * 32:(hh + 1) * 32, cq] = ls[4 * cq + hh]

    if "nc" not in _CACHE:
        _CACHE["nc"] = build_program()
    nc = _CACHE["nc"]

    f32c = lambda a: np.ascontiguousarray(np.asarray(a, np.float32))
    weights = {"wqs": f32c(L0_attn_Wq_sem), "wks": f32c(L0_attn_Wk_sem),
               "wqg": f32c(L0_attn_Wq_geo), "wkg": f32c(L0_attn_Wk_geo),
               "wv": f32c(L0_attn_Wv), "wo": f32c(L0_attn_Wo),
               "gw": f32c(L0_ffn_gate_w), "w1": f32c(L0_ffn_w1),
               "w2": f32c(L0_ffn_w2), "sw1": f32c(L0_ffn_sw1),
               "sw2": f32c(L0_ffn_sw2)}
    in_maps, own_cols_all = make_in_maps(x, weights, lsx)
    res = run_bass_kernel_spmd(nc, in_maps, core_ids=list(range(8)))
    y = np.zeros((B, T, D), np.float32)
    for c in range(8):
        b = c // 4
        y[b][own_cols_all[c]] = res.results[c]["y"]
    return y
